# version 09
# baseline (speedup 1.0000x reference)
import numpy as np
from contextlib import ExitStack

import concourse.bass as bass
import concourse.mybir as mybir
from concourse.bass_utils import run_bass_kernel_spmd

F32 = mybir.dt.float32
BF16 = mybir.dt.bfloat16
I32 = mybir.dt.int32
AX = mybir.AluOpType
AF = mybir.ActivationFunctionType
AXL = mybir.AxisListType

B, L, D = 4, 4096, 1024
HID = 3072
FD, HD = 16, 64
TOK = 2048
NCHO = 16
NCH = 32
NJ = HID // 128
EPS = 1e-5
ZK = 1.0 / 256.0
QSC = FD ** -0.5
FSC = 0.5 / (127.0 * 127.0)

SKIP_ATTN = False
PHASES = "all"
LNCH = 8
GROUPS = [[0, 1], [2, 3], [4, 5], [6, 7]]


class K:

    def __init__(self, nc, block, S):
        self.nc, self.block, self.S = nc, block, S
        self.cnt = {k: 0 for k in S}
        self.trfree = {}

    def inc(self, inst, key, k=1):
        inst.then_inc(self.S[key], k)
        self.cnt[key] += k
        return self.cnt[key]

    def bump(self, key, k=1):
        self.cnt[key] += k
        return self.cnt[key]

    @staticmethod
    def w(e, S, key, n):
        if n > 0:
            e.wait_ge(S[key], n)


def build_nc():
    nc = bass.Bass("TRN2", target_bir_lowering=False, detect_race_conditions=False)

    x_in = nc.dram_tensor("x_own", [TOK, D], F32, kind="ExternalInput").ap()
    wq_n = nc.dram_tensor("wq_own", [128, D], F32, kind="ExternalInput").ap()
    wk_n = nc.dram_tensor("wk_own", [128, D], F32, kind="ExternalInput").ap()
    wv_n = nc.dram_tensor("wv_own", [512, D], F32, kind="ExternalInput").ap()
    wo_n = nc.dram_tensor("wo", [D, D], F32, kind="ExternalInput").ap()
    gam_in = nc.dram_tensor("gamma", [FD], F32, kind="ExternalInput").ap()
    bet_in = nc.dram_tensor("beta", [FD], F32, kind="ExternalInput").ap()
    wg_n = nc.dram_tensor("w_gate", [HID, D], I32, kind="ExternalInput").ap()
    wu_n = nc.dram_tensor("w_up", [HID, D], I32, kind="ExternalInput").ap()
    wd_n = nc.dram_tensor("w_down", [D, HID], I32, kind="ExternalInput").ap()
    y_out = nc.dram_tensor("y_own", [TOK, D], F32, kind="ExternalOutput").ap()

    wgT_sc = nc.dram_tensor("wgT_sc", [NJ, 8, 128, 128], BF16).ap()
    wuT_sc = nc.dram_tensor("wuT_sc", [NJ, 8, 128, 128], BF16).ap()
    wdT_sc = nc.dram_tensor("wdT_sc", [NJ, 8, 128, 128], BF16).ap()
    cc_x1_in = nc.dram_tensor("cc_x1_in", [2, D, 1024], BF16).ap()
    cc_x1_outh = [nc.dram_tensor(f"cc_x1_out{h}", [2 * D, 1024], BF16).ap()
                  for h in range(2)]
    cc_o_in = nc.dram_tensor("cc_o_in", [L, 512], BF16).ap()
    cc_o_out = [nc.dram_tensor(f"cc_o_out{q}", [2048, 512], BF16).ap() for q in range(4)]

    es = ExitStack()

    def sb(e, name, shape, dt=BF16):
        return e.enter_context(nc.sbuf_tensor(name, shape, dt))

    with es:
        x_res = sb(es, "x_res", [128, NCHO * D])
        xT = sb(es, "xT", [128, 8 * TOK])
        xnbuf = sb(es, "xnbuf", [128, 2 * 1024])
        wstage = sb(es, "wstage", [128, 6 * 1024])
        sq_scr = sb(es, "sq_scr", [128, 1024], F32)
        lnsq = sb(es, "lnsq", [128, 1024])
        lnstat = sb(es, "lnstat", [128, 128], F32)
        outst = sb(es, "outst", [128, 2 * 1024], F32)
        oev = sb(es, "oev", [128, 2 * 512])
        ident = sb(es, "ident", [128, 128])
        trimask = sb(es, "trimask", [128, 128])
        gb = sb(es, "gb", [128, 4 * FD], F32)
        gbr = sb(es, "gbr", [128, 2 * FD], F32)
        zbuf = sb(es, "zbuf", [128, 16], F32)
        fmst = sb(es, "fmst", [128, 96], F32)
        qkbuf = sb(es, "qkbuf", [128, 256], F32)
        cst = sb(es, "cst", [128, 4], F32)

        p_trA = es.enter_context(nc.psum_tensor("p_trA", [128, 1024], BF16))
        p_trB = es.enter_context(nc.psum_tensor("p_trB", [128, 1024], BF16))
        p_mm = [es.enter_context(nc.psum_tensor(f"p_mm{i}", [128, 512], F32))
                for i in range(6)]
        p_qk, p_vs = p_mm[0], p_mm[1]
        p_S = [p_mm[2], p_mm[3], p_mm[4]]
        p_o = p_mm[5]

        semnames = ["pe", "dv", "ac", "gp", "dy", "dg", "do", "cc"]
        S = {k: es.enter_context(nc.semaphore("s_" + k)) for k in semnames}
        block = es.enter_context(nc.Block())
        nc.cache_partition_id()
        for ci, cv in enumerate((EPS, 1.0 / 127.0, 0.0, 1.0)):
            nc.const_aps.register(F32, cv, cst[:, ci:ci + 1])
        kk = K(nc, block, S)
        cnt = kk.cnt

        constgp = [0]

        def w(e, key, n):
            if n > 0:
                e.wait_ge(S[key], n)

        def pe(f): block.tensor(f)
        def dv(f): block.vector(f)
        def ac(f): block.scalar(f)
        def gp(f): block.gpsimd(f)
        def sy(f): block.sync(f)

        def tr_group(items, bank, evac_eng, extra_waits=(), batch_evac=None):
            pbank = p_trA if bank == "A" else p_trB
            free = kk.trfree.get(bank)
            waits = list(extra_waits)
            if free is not None:
                waits.append(free)

            def ft(t, items=items, pbank=pbank, waits=tuple(waits)):
                w(t, "gp", constgp[0])
                for key, n in waits:
                    w(t, key, n)
                li = None
                for i, (src, _d, _o) in enumerate(items):
                    li = t.transpose(pbank[:, i * 128:(i + 1) * 128], src, ident[:])
                li.then_inc(S["pe"], 1)
                cnt["pe"] += 1
            pe(ft)
            pe_n = cnt["pe"]

            if PHASES == "lnt":
                res = ("dv", 0)
                kk.trfree[bank] = ("pe", pe_n)
                return res, pe_n
            if evac_eng == "dv":
                def fe(v, items=items, pbank=pbank, n=pe_n, batch=batch_evac):
                    w(v, "pe", n)
                    if batch is not None:
                        li = v.tensor_copy(batch, pbank[:, 0:128 * len(items)])
                    else:
                        for i, (_s, dstb, off) in enumerate(items):
                            li = v.tensor_copy(dstb[:, off:off + 128],
                                               pbank[:, i * 128:(i + 1) * 128])
                    li.then_inc(S["dv"], 1)
                    cnt["dv"] += 1
                dv(fe)
                res = ("dv", cnt["dv"])
            else:
                def fe(s, items=items, pbank=pbank, n=pe_n, batch=batch_evac):
                    w(s, "pe", n)
                    if batch is not None:
                        li = s.copy(batch, pbank[:, 0:128 * len(items)])
                    else:
                        for i, (_s2, dstb, off) in enumerate(items):
                            li = s.copy(dstb[:, off:off + 128],
                                        pbank[:, i * 128:(i + 1) * 128])
                    li.then_inc(S["ac"], 1)
                    cnt["ac"] += 1
                ac(fe)
                res = ("ac", cnt["ac"])
            kk.trfree[bank] = res
            return res, pe_n

        def f00(v):
            for ci, cv in enumerate((EPS, 1.0 / 127.0, 0.0, 1.0)):
                v.memset(cst[:, ci:ci + 1], cv)
            if PHASES == "lnh0":
                v.memset(lnstat[:], 777.0)
            v.memset(ident[:], 1.0)
            v.memset(trimask[:], 1.0).then_inc(S["dv"], 1)
            cnt["dv"] += 1
        dv(f00)
        pre_dv = cnt["dv"]

        def f0(g):
            g.dma_start(
                out=bass.AP(x_res, 0, [[NCHO * D, 128], [D, NCHO], [1, D]]),
                in_=bass.AP(x_in.tensor, 0, [[D, 128], [128 * D, NCHO], [1, D]]),
                ).then_inc(S["dg"], 16)
            cnt["dg"] += 16
            g.dma_start(out=bass.AP(gbr, 0, [[2 * FD, 128], [1, FD]]),
                        in_=bass.AP(gam_in.tensor, 0, [[0, 128], [1, FD]])).then_inc(S["dg"], 16)
            cnt["dg"] += 16
            g.dma_start(out=bass.AP(gbr, FD, [[2 * FD, 128], [1, FD]]),
                        in_=bass.AP(bet_in.tensor, 0, [[0, 128], [1, FD]])).then_inc(S["dg"], 16)
            cnt["dg"] += 16
            w(g, "dv", pre_dv)
            g.affine_select(out=ident[:], in_=ident[:], pattern=[[-1, 128]],
                            compare_op=AX.is_equal, fill=0.0, base=0,
                            channel_multiplier=1).then_inc(S["gp"], 1)
            cnt["gp"] += 1
            g.wait_ge(S["gp"], cnt["gp"])
            g.affine_select(out=trimask[:], in_=trimask[:], pattern=[[1, 128]],
                            compare_op=AX.is_ge, fill=0.0, base=0,
                            channel_multiplier=-1).then_inc(S["gp"], 1)
            cnt["gp"] += 1
        gp(f0)
        x_dg = 16
        gb_dg = 48
        const_gp = cnt["gp"]
        constgp[0] = const_gp

        def f0v(v):
            w(v, "dg", gb_dg)
            v.tensor_scalar(gb[:, 0:FD], gbr[:, 0:FD], QSC, 0.0, AX.mult, AX.add)
            v.tensor_scalar(gb[:, FD:2 * FD], gbr[:, FD:2 * FD], QSC, 0.0, AX.mult, AX.add)
            v.tensor_copy(gb[:, 2 * FD:3 * FD], gbr[:, 0:FD])
            v.tensor_copy(gb[:, 3 * FD:4 * FD], gbr[:, FD:2 * FD]).then_inc(S["dv"], 1)
            cnt["dv"] += 1
        dv(f0v)
        gb_dv = cnt["dv"]

        def emit_ln_half(hf, dep):
            lb = hf * 64
            nch = LNCH if PHASES in ("lnh0", "lnr", "lns", "lna", "lnt") else 8
            for t in range(nch):
                tg = hf * 8 + t
                xc = x_res[:, tg * D:(tg + 1) * D]
                dk, dn = dep(tg)

                def fg(v, t=t, xc=xc, dk=dk, dn=dn):
                    w(v, dk, dn)
                    v.reduce_sum(lnstat[:, lb + t:lb + t + 1], xc, axis=AXL.X)
                    v.tensor_mul(lnsq[:], xc, xc)
                    v.reduce_sum(lnstat[:, lb + 8 + t:lb + 9 + t], lnsq[:],
                                 axis=AXL.X).then_inc(S["dv"], 1)
                    cnt["dv"] += 1
                dv(fg)
            if PHASES == "lnr":
                return ("dv", cnt["dv"]), ("ac", cnt["ac"])

            def fv(v):
                mu, var = lnstat[:, lb + 16:lb + 24], lnstat[:, lb + 24:lb + 32]
                v.drain()
                v.tensor_scalar(mu, lnstat[:, lb + 0:lb + 8], 1.0 / D, 0.0, AX.mult, AX.add)
                v.tensor_scalar(var, lnstat[:, lb + 8:lb + 16], 1.0 / D, 0.0, AX.mult, AX.add)
                v.tensor_scalar(lnstat[:, lb + 56:lb + 64], lnstat[:, lb + 0:lb + 8],
                                -1.0 / D, 0.0, AX.mult, AX.add)
                v.drain()
                v.tensor_mul(lnstat[:, lb + 32:lb + 40], mu, mu)
                v.drain()
                v.tensor_sub(var, var, lnstat[:, lb + 32:lb + 40])
                if PHASES == "lnh0":
                    v.drain()
                    v.tensor_copy(lnstat[:, 80:88], var)
                v.drain().then_inc(S["dv"], 1)
                cnt["dv"] += 1
            dv(fv)

            def fs(s, n=cnt["dv"]):
                w(s, "dv", n)
                if PHASES == "lnh0":
                    s.activation(lnstat[:, 72:80], lnstat[:, lb + 24:lb + 32],
                                 AF.Copy, bias=0.0, scale=1.0)
                s.activation(lnstat[:, lb + 40:lb + 48], lnstat[:, lb + 24:lb + 32], AF.Sqrt,
                             bias=EPS, scale=1.0)
                s.drain().then_inc(S["ac"], 1)
                cnt["ac"] += 1
            ac(fs)

            def fv2(v, n=cnt["ac"]):
                w(v, "ac", n)
                inv = lnstat[:, lb + 48:lb + 56]
                v.reciprocal(inv, lnstat[:, lb + 40:lb + 48])
                v.drain()
                v.tensor_mul(lnstat[:, lb + 32:lb + 40], lnstat[:, lb + 56:lb + 64],
                             inv)
                v.drain().then_inc(S["dv"], 1)
                cnt["dv"] += 1
            dv(fv2)
            inv_dv = cnt["dv"]
            if PHASES == "lns":
                return ("dv", cnt["dv"]), ("ac", cnt["ac"])

            for t in range(nch):
                tg = hf * 8 + t
                xc = x_res[:, tg * D:(tg + 1) * D]
                slot = t % 2

                def fv3(v, xc=xc, t=t, slot=slot, n=inv_dv):
                    w(v, "dv", n)
                    v.tensor_scalar(xnbuf[:, slot * 1024:(slot + 1) * 1024], xc,
                                    lnstat[:, lb + 48 + t:lb + 49 + t], lnstat[:, lb + 32 + t:lb + 33 + t],
                                    AX.mult, AX.add).then_inc(S["dv"], 1)
                    cnt["dv"] += 1
                dv(fv3)
                ap_dv = cnt["dv"]
                if PHASES == "lna":
                    continue

                items = [(xnbuf[:, slot * 1024 + dc * 128: slot * 1024 + (dc + 1) * 128],
                          xT, dc * TOK + hf * 1024 + t * 128) for dc in range(8)]
                tr_group(items, "A" if slot == 0 else "B",
                         "dv" if slot == 0 else "ac",
                         extra_waits=[("dv", ap_dv)])
            return (kk.trfree.get("A", ("dv", 0)), kk.trfree.get("B", ("ac", 0)))

        wsslot_user = {}

        def stage_load(slot, src, cast_queue="dg"):
            prev = wsslot_user.get(slot)

            def fy(g, slot=slot, src=src, prev=prev):
                if prev is not None:
                    w(g, prev[0], prev[1])
                g.dma_start(out=wstage[:, slot * 1024:(slot + 1) * 1024],
                            in_=src).then_inc(S["dg"], 16)
                cnt["dg"] += 16
            gp(fy)
            return cnt["dg"]

        spill_do = {}

        def wslab_transpose(kind, j, dstb, dst_off, spill_sc, slot, busy_wait):
            if kind == "g":
                src = bass.AP(wg_n.tensor, j * 128 * D, [[D, 128], [1, D]])
            elif kind == "u":
                src = bass.AP(wu_n.tensor, j * 128 * D, [[D, 128], [1, D]])
            else:
                src = bass.AP(wd_n.tensor, j * 128, [[HID, 128], [128 * HID, 8], [1, 128]])
            dg_n = stage_load(slot, src)
            stage = wstage[:, slot * 1024:(slot + 1) * 1024]
            items = [(stage[:, dc * 128:(dc + 1) * 128], dstb, dst_off + dc * 128)
                     for dc in range(8)]
            ew = [("dg", dg_n)]
            psp = spill_do.get((kind, dst_off))
            if psp is not None:
                ew.append(("do", psp))
            if busy_wait is not None:
                ew.append(busy_wait)
            (ek, en), pe_n = tr_group(items, "A", "dv", extra_waits=ew,
                                      batch_evac=dstb[:, dst_off:dst_off + 1024])
            wsslot_user[slot] = ("pe", pe_n)

            def fsp(y, dstb=dstb, dst_off=dst_off, spill_sc=spill_sc, j=j, en=en):
                w(y, "dv", en)
                y.dma_start(out=bass.AP(spill_sc.tensor, j * 8 * 128 * 128,
                                        [[128, 128], [128 * 128, 8], [1, 128]]),
                            in_=dstb[:, dst_off:dst_off + 1024]).then_inc(S["do"], 16)
                cnt["do"] += 16
            sy(fsp)
            spill_do[(kind, dst_off)] = cnt["do"]
            return ("dv", en)

        def wslab_stream(sc, j, dstb, dst_off, busy_wait, spill_do):
            def fy(y, sc=sc, j=j, dstb=dstb, dst_off=dst_off,
                   busy_wait=busy_wait, spill_do=spill_do):
                w(y, "do", spill_do)
                if busy_wait is not None:
                    w(y, busy_wait[0], busy_wait[1])
                y.dma_start(out=dstb[:, dst_off:dst_off + 1024],
                            in_=bass.AP(sc.tensor, j * 8 * 128 * 128,
                                        [[128, 128], [128 * 128, 8], [1, 128]]),
                            ).then_inc(S["dy"], 16)
                cnt["dy"] += 16
            sy(fy)
            return ("dy", cnt["dy"])

        spill_done = {"g": 0, "u": 0, "d": 0}

        def emit_ffn(call, bufs, dep, prev_mm_free):
            if PHASES == "p0":
                return {t: cnt["dv"] for t in range(16)}, cnt["dv"]
            """call in {1, 2}; bufs: dict with Hh, wgT_b, wuT_b, wdT_f.
            dep(tg): gate for x_res chunk reads. prev_mm_free: (key, n) that
            must pass before first use of p_mm banks. Returns per-chunk evac
            counts (for call 1: x_res update dv counts)."""
            Hh, wgT_b, wuT_b, wdT_f = bufs["Hh"], bufs["wgT_b"], bufs["wuT_b"], bufs["wdT_f"]
            lnA = emit_ln_half(0, dep)
            if PHASES in ("lnh0", "lnr", "lns", "lna", "lnt"):
                return {t: cnt["dv"] for t in range(16)}, cnt["dv"]
            lnB = emit_ln_half(1, dep)
            ln_dv = cnt["dv"]
            ln_ac = cnt["ac"]
            x_upd = {}
            if PHASES in ("ln1", "lnh0", "lnr", "lns", "lna", "lnt"):
                return {t: cnt["dv"] for t in range(16)}, cnt["dv"]

            gu_mm_pe = {}
            silu_ac = {}
            hmul_dv = {}
            last_down_ev = [0]

            for hf in range(2):
                for j in range(NJ):
                    bslot = j % 3
                    jp3 = (hf, j - 3) if j >= 3 else ((hf - 1, NJ - 3 + j) if hf == 1 else None)
                    busy = ("pe", gu_mm_pe[jp3]) if jp3 in gu_mm_pe else None
                    if call == 2 and busy is None and prev_mm_free is not None:
                        busy = prev_mm_free
                    if call == 1 and hf == 0:
                        wg_r = wslab_transpose("g", j, wgT_b, bslot * 1024, wgT_sc,
                                               (3 * j) % 6, busy)
                        wu_r = wslab_transpose("u", j, wuT_b, bslot * 1024,
                                               wuT_sc, (3 * j + 1) % 6, busy)
                        wd_r = wslab_transpose("d", j, wdT_f, j * 1024, wdT_sc,
                                               (3 * j + 2) % 6, None)
                        spill_done["g"] = spill_done["u"] = spill_done["d"] = cnt["do"]
                        bufs.setdefault("wd_ready", {})[j] = wd_r
                    elif hf == 0:
                        wg_r = wslab_stream(wgT_sc, j, wgT_b, bslot * 1024, busy,
                                            spill_done["g"])
                        wu_r = wslab_stream(wuT_sc, j, wuT_b, bslot * 1024,
                                            busy, spill_done["u"])
                        wd_r = wslab_stream(wdT_sc, j, wdT_f, j * 1024,
                                            prev_mm_free if j == 0 else None,
                                            spill_done["d"])
                        bufs.setdefault("wd_ready", {})[j] = wd_r
                    else:
                        wg_r = wslab_stream(wgT_sc, j, wgT_b, bslot * 1024, busy,
                                            spill_done["g"])
                        wu_r = wslab_stream(wuT_sc, j, wuT_b, bslot * 1024,
                                            busy, spill_done["u"])

                    def ft(t, j=j, hf=hf, bslot=bslot, wg_r=wg_r, wu_r=wu_r,
                           ln_dv=ln_dv, ln_ac=ln_ac, first=(j == 0)):
                        w(t, wg_r[0], wg_r[1])
                        w(t, wu_r[0], wu_r[1])
                        if first:
                            w(t, "dv", ln_dv)
                            w(t, "ac", ln_ac)
                            w(t, "dv", last_down_ev[0])
                            if hf == 0 and prev_mm_free is not None:
                                w(t, prev_mm_free[0], prev_mm_free[1])
                        jp = (hf, j - 1) if j >= 1 else (hf - 1, NJ - 1)
                        if jp in silu_ac:
                            w(t, "ac", silu_ac[jp])
                        if jp in hmul_dv:
                            w(t, "dv", hmul_dv[jp])
                        li = None
                        for which in range(2):
                            wb = wgT_b if which == 0 else wuT_b
                            for dc in range(8):
                                lhs = wb[:, bslot * 1024 + dc * 128:
                                         bslot * 1024 + (dc + 1) * 128]
                                for blk in range(2):
                                    rhs = xT[:, dc * TOK + hf * 1024 + blk * 512:
                                             dc * TOK + hf * 1024 + (blk + 1) * 512]
                                    li = t.matmul(p_mm[2 * which + blk][:], lhs, rhs,
                                                  start=(dc == 0), stop=(dc == 7))
                        li.then_inc(S["pe"], 1)
                        cnt["pe"] += 1
                    pe(ft)
                    gu_mm_pe[(hf, j)] = cnt["pe"]
                    gu_pe = cnt["pe"]

                    def fs(s, n=gu_pe, hp=hmul_dv.get((hf, j - 1) if j >= 1
                                                        else (hf - 1, NJ - 1), 0)):
                        w(s, "pe", n)
                        w(s, "dv", hp)
                        li = None
                        for blk in range(2):
                            li = s.activation(sq_scr[:, blk * 512:(blk + 1) * 512],
                                              p_mm[blk][:], AF.Silu, scale=1.0 / 127.0)
                        li.then_inc(S["ac"], 1)
                        cnt["ac"] += 1
                    ac(fs)
                    silu_ac[(hf, j)] = cnt["ac"]

                    def fv(v, j=j, n_ac=cnt["ac"], n_pe=gu_pe):
                        w(v, "ac", n_ac)
                        w(v, "pe", n_pe)
                        li = None
                        for blk in range(2):
                            li = v.tensor_mul(Hh[:, j * 1024 + blk * 512:
                                                 j * 1024 + (blk + 1) * 512],
                                              sq_scr[:, blk * 512:(blk + 1) * 512],
                                              p_mm[2 + blk][:])
                        li.then_inc(S["dv"], 1)
                        cnt["dv"] += 1
                    dv(fv)
                    hmul_dv[(hf, j)] = cnt["dv"]

                if PHASES in ("hh", "wdt") and hf == 0 and call == 1:
                    def fdh(g, nd=cnt["dv"], na=cnt["ac"], ndo=cnt["do"]):
                        w(g, "dv", nd)
                        w(g, "ac", na)
                        w(g, "do", ndo)
                        src = Hh if PHASES == "hh" else wdT_f
                        v32 = src.bitcast(F32)
                        g.dma_start(
                            out=bass.AP(y_out.tensor, 0, [[12288, 128], [1, 12288]]),
                            in_=v32[:, 0:12288]).then_inc(S["dg"], 16)
                        cnt["dg"] += 16
                    gp(fdh)
                    return {t: cnt["dv"] for t in range(16)}, cnt["dv"]

                down_ev = {}
                for grp in range(4):
                    def ftd(t, hf=hf, grp=grp,
                            h_last=hmul_dv[(hf, NJ - 1)], s_last=silu_ac[(hf, NJ - 1)]):
                        if grp == 0:
                            w(t, "dv", h_last)
                            w(t, "ac", s_last)
                        else:
                            w(t, "dv", down_ev[grp - 1])
                        for j in range(NJ):
                            wd_r = bufs["wd_ready"][j]
                            w(t, wd_r[0], wd_r[1])
                            for ch in range(2):
                                tl = grp * 2 + ch
                                lhs = Hh[:, j * 1024 + tl * 128: j * 1024 + (tl + 1) * 128]
                                for dh in range(2):
                                    rhs = wdT_f[:, j * 1024 + dh * 512:
                                                j * 1024 + (dh + 1) * 512]
                                    li = t.matmul(p_mm[ch * 2 + dh][:], lhs, rhs,
                                                  start=(j == 0), stop=(j == NJ - 1))
                        li.then_inc(S["pe"], 1)
                        cnt["pe"] += 1
                    pe(ftd)
                    d_pe = cnt["pe"]

                    def fvd(v, hf=hf, grp=grp, n=d_pe, call=call):
                        w(v, "pe", n)
                        for ch in range(2):
                            tg = hf * 8 + grp * 2 + ch
                            if call == 2:
                                w(v, "do", outdma.get(tg - 2, 0))
                            for dh in range(2):
                                pmm = p_mm[ch * 2 + dh][:]
                                xsl = x_res[:, tg * D + dh * 512: tg * D + (dh + 1) * 512]
                                if call == 1:
                                    li = v.scalar_tensor_tensor(xsl, pmm, FSC, xsl,
                                                                AX.mult, AX.add)
                                else:
                                    li = v.scalar_tensor_tensor(
                                        outst[:, (tg % 2) * 1024 + dh * 512:
                                              (tg % 2) * 1024 + (dh + 1) * 512],
                                        pmm, FSC, xsl, AX.mult, AX.add)
                        li.then_inc(S["dv"], 1)
                        cnt["dv"] += 1
                    dv(fvd)
                    down_ev[grp] = cnt["dv"]
                    last_down_ev[0] = cnt["dv"]
                    for ch in range(2):
                        x_upd[hf * 8 + grp * 2 + ch] = cnt["dv"]

                    if call == 2:
                        def fyo(y, hf=hf, grp=grp, n=cnt["dv"]):
                            w(y, "dv", n)
                            for ch in range(2):
                                tg = hf * 8 + grp * 2 + ch
                                y.dma_start(
                                    out=bass.AP(y_out.tensor, tg * 128 * D,
                                                [[D, 128], [1, D]]),
                                    in_=outst[:, (tg % 2) * 1024:(tg % 2 + 1) * 1024],
                                ).then_inc(S["do"], 16)
                                cnt["do"] += 16
                                outdma[tg] = cnt["do"]
                        sy(fyo)
            return x_upd, cnt["dv"]

        outdma = {}

        ffn_es1 = ExitStack()
        if PHASES in ("ln1", "lnh0"):
            bufs1 = {
                "Hh": sb(ffn_es1, "Hh1", [128, 128]),
                "wgT_b": sb(ffn_es1, "wgT1", [128, 128]),
                "wuT_b": sb(ffn_es1, "wuT1", [128, 128]),
                "wdT_f": sb(ffn_es1, "wdT1", [128, 128]),
            }
        else:
            bufs1 = {
                "Hh": sb(ffn_es1, "Hh1", [128, NJ * 1024]),
                "wgT_b": sb(ffn_es1, "wgT1", [128, 3 * 1024]),
                "wuT_b": sb(ffn_es1, "wuT1", [128, 3 * 1024]),
                "wdT_f": sb(ffn_es1, "wdT1", [128, NJ * 1024]),
            }
        x1_upd, ffn1_dv = emit_ffn(1, bufs1, lambda tg: ("dg", x_dg), None)
        if PHASES in ("ffn1", "ln1", "lnh0", "p0", "lnr", "lns", "lna", "lnt",
                      "hh", "wdt"):
            if PHASES == "lnh0":
                def fvend(v, na=cnt["ac"]):
                    w(v, "ac", na)
                    v.tensor_copy(lnstat[:, 96:104], lnstat[:, 24:32]).then_inc(S["dv"], 1)
                    cnt["dv"] += 1
                dv(fvend)

            def fdum(g, nd=cnt["dv"], na=cnt["ac"]):
                w(g, "dv", nd)
                w(g, "ac", na)
                if PHASES in ("lnh0", "ln1"):
                    g.dma_start(
                        out=bass.AP(y_out.tensor, 0, [[8192, 128], [1, 128]]),
                        in_=lnstat[:],
                    ).then_inc(S["dg"], 16)
                    cnt["dg"] += 16
                    g.dma_start(
                        out=bass.AP(y_out.tensor, 128, [[8192, 128], [1, 2048]]),
                        in_=xnbuf[:],
                    ).then_inc(S["dg"], 16)
                    cnt["dg"] += 16
                    g.dma_start(
                        out=bass.AP(y_out.tensor, 128 + 2048, [[8192, 128], [1, 4096]]),
                        in_=xT[:, 0:4096],
                    ).then_inc(S["dg"], 16)
                    cnt["dg"] += 16
                elif PHASES not in ("hh", "wdt"):
                    g.dma_start(
                        out=bass.AP(y_out.tensor, 0, [[D, 128], [128 * D, NCHO], [1, D]]),
                        in_=bass.AP(x_res, 0, [[NCHO * D, 128], [D, NCHO], [1, D]]),
                    ).then_inc(S["dg"], 16)
                    cnt["dg"] += 16
            gp(fdum)

            def ffin1(y, n=cnt["do"], nd=cnt["dg"]):
                y.wait_ge(S["do"], n)
                y.wait_ge(S["dg"], nd)
            sy(ffin1)
            ffn_es1.close()
            return nc
        ffn_es1.close()

        x2_upd = x1_upd

        if not SKIP_ATTN:
            attn_es = ExitStack()
            wqT = sb(attn_es, "wqT", [128, 8 * 128])
            wkT = sb(attn_es, "wkT", [128, 8 * 128])
            wvT = sb(attn_es, "wvT", [128, 8 * 512])
            woT = sb(attn_es, "woT", [128, 8 * 1024])
            x1t = sb(attn_es, "x1t", [128, 3 * 1024])
            qfm = sb(attn_es, "qfm", [128, 2 * 128])
            kfm = sb(attn_es, "kfm", [128, 2 * 128])
            phi = sb(attn_es, "phi", [128, 2 * 2048])
            psi = sb(attn_es, "psi", [128, 2 * 2048])
            phiT = sb(attn_es, "phiT", [128, 2048])
            psiT = sb(attn_es, "psiT", [128, 2048])
            vts = sb(attn_es, "vts", [128, 2 * 520])
            s2m = sb(attn_es, "s2m", [128, 1024])
            Ssb = sb(attn_es, "Ssb", [128, 1536])
            o_asm8 = sb(attn_es, "o_asm8", [128, 8192])
            oT = sb(attn_es, "oT", [128, 1024])

            def emit_ln2():
                for hf in range(2):
                    lb = hf * 64
                    for t in range(8):
                        tg = hf * 8 + t
                        xc = x_res[:, tg * D:(tg + 1) * D]

                        def fg(v, t=t, xc=xc, n=x1_upd[hf * 8 + t]):
                            w(v, "dv", n)
                            v.reduce_sum(lnstat[:, lb + t:lb + t + 1], xc, axis=AXL.X)
                            v.tensor_mul(lnsq[:], xc, xc)
                            v.reduce_sum(lnstat[:, lb + 8 + t:lb + 9 + t], lnsq[:],
                                         axis=AXL.X).then_inc(S["dv"], 1)
                            cnt["dv"] += 1
                        dv(fg)

                    def fv(v):
                        mu, var = lnstat[:, lb + 16:lb + 24], lnstat[:, lb + 24:lb + 32]
                        v.drain()
                        v.tensor_scalar(mu, lnstat[:, lb + 0:lb + 8], 1.0 / D, 0.0, AX.mult, AX.add)
                        v.tensor_scalar(var, lnstat[:, lb + 8:lb + 16], 1.0 / D, 0.0, AX.mult, AX.add)
                        v.tensor_scalar(lnstat[:, lb + 56:lb + 64], lnstat[:, lb + 0:lb + 8],
                                        -1.0 / D, 0.0, AX.mult, AX.add)
                        v.drain()
                        v.tensor_mul(lnstat[:, lb + 32:lb + 40], mu, mu)
                        v.drain()
                        v.tensor_sub(var, var, lnstat[:, lb + 32:lb + 40])
                        v.drain().then_inc(S["dv"], 1)
                        cnt["dv"] += 1
                    dv(fv)

                    def fs(s, n=cnt["dv"]):
                        w(s, "dv", n)
                        s.activation(lnstat[:, lb + 40:lb + 48], lnstat[:, lb + 24:lb + 32], AF.Sqrt,
                                     bias=EPS, scale=1.0)
                        s.drain().then_inc(S["ac"], 1)
                        cnt["ac"] += 1
                    ac(fs)

                    def fv2(v, n=cnt["ac"]):
                        w(v, "ac", n)
                        inv = lnstat[:, lb + 48:lb + 56]
                        v.reciprocal(inv, lnstat[:, lb + 40:lb + 48])
                        v.drain()
                        v.tensor_mul(lnstat[:, lb + 32:lb + 40], lnstat[:, lb + 56:lb + 64],
                                     inv)
                        v.drain().then_inc(S["dv"], 1)
                        cnt["dv"] += 1
                    dv(fv2)
                    inv_dv = cnt["dv"]

                    slot_dma = {}
                    for t in range(8):
                        tg = hf * 8 + t
                        xc = x_res[:, tg * D:(tg + 1) * D]
                        slot = t % 2

                        def fv3(v, xc=xc, t=t, slot=slot, n=inv_dv,
                                pdma=slot_dma.get(t - 2, 0)):
                            w(v, "dv", n)
                            w(v, "do", pdma)
                            v.tensor_scalar(xnbuf[:, slot * 1024:(slot + 1) * 1024], xc,
                                            lnstat[:, lb + 48 + t:lb + 49 + t],
                                            lnstat[:, lb + 32 + t:lb + 33 + t],
                                            AX.mult, AX.add).then_inc(S["dv"], 1)
                            cnt["dv"] += 1
                        dv(fv3)
                        ap_dv = cnt["dv"]

                        items = [(xnbuf[:, slot * 1024 + dc * 128:
                                        slot * 1024 + (dc + 1) * 128],
                                  xnbuf, slot * 1024 + dc * 128) for dc in range(8)]
                        (ek, en), _ = tr_group(
                            items, "A" if slot == 0 else "B",
                            "dv" if slot == 0 else "ac",
                            extra_waits=[("dv", ap_dv)],
                            batch_evac=xnbuf[:, slot * 1024:(slot + 1) * 1024])

                        def fy(y, tg=tg, hf=hf, slot=slot, ek=ek, en=en):
                            w(y, ek, en)
                            y.dma_start(
                                out=bass.AP(cc_x1_in.tensor,
                                            hf * D * 1024 + (tg % 8) * 128,
                                            [[1024, 128], [128 * 1024, 8], [1, 128]]),
                                in_=xnbuf[:, slot * 1024:(slot + 1) * 1024],
                            ).then_inc(S["do"], 16)
                            cnt["do"] += 16
                        sy(fy)
                        slot_dma[t] = cnt["do"]

                    def fcc(g, hf=hf, n=cnt["do"]):
                        w(g, "do", n)
                        g.collective_compute(
                            "AllGather", AX.bypass, replica_groups=GROUPS,
                            ins=[bass.AP(cc_x1_in.tensor, hf * D * 1024,
                                         [[1024, D], [1, 1024]])],
                            outs=[cc_x1_outh[hf]]).then_inc(S["cc"], 1)
                        cnt["cc"] += 1
                    gp(fcc)

            emit_ln2()
            x1_ag = cnt["cc"]

            if PHASES == "x1dump":
                def fx1(g, n=x1_ag):
                    w(g, "cc", n)
                    g.dma_start(
                        out=bass.AP(y_out.tensor, 0, [[2048, 1024], [1, 2048]]),
                        in_=bass.AP(cc_x1_outh[0].tensor, 0, [[1024, 2048], [1, 1024]]),
                    ).then_inc(S["dg"], 16)
                    cnt["dg"] += 16
                gp(fx1)

                def ffx1(y, n=cnt["do"], nd=cnt["dg"]):
                    y.wait_ge(S["do"], n)
                    y.wait_ge(S["dg"], nd)
                sy(ffx1)
                attn_es.close()
                return nc

            def emit_attn_w():
                jobs = []
                jobs.append((bass.AP(wq_n.tensor, 0, [[D, 128], [1, D]]), wqT,
                             [(dc, dc * 128) for dc in range(8)]))
                jobs.append((bass.AP(wk_n.tensor, 0, [[D, 128], [1, D]]), wkT,
                             [(dc, dc * 128) for dc in range(8)]))
                for s_ in range(4):
                    jobs.append((bass.AP(wv_n.tensor, s_ * 128 * D, [[D, 128], [1, D]]),
                                 wvT, [(dc, dc * 512 + s_ * 128) for dc in range(8)]))
                for dcs in range(8):
                    jobs.append((bass.AP(wo_n.tensor, dcs * 128 * D, [[D, 128], [1, D]]),
                                 woT, [(ofc, ofc * 1024 + dcs * 128) for ofc in range(8)]))
                for ji, (src, dstb, tiles) in enumerate(jobs):
                    slot = ji % 6
                    dg_n = stage_load(slot, src)
                    stage = wstage[:, slot * 1024:(slot + 1) * 1024]
                    items = [(stage[:, sc * 128:(sc + 1) * 128], dstb, off)
                             for (sc, off) in tiles]
                    (_ek, _en), pe_n = tr_group(items, "A" if ji % 2 == 0 else "B",
                                                "dv" if ji % 2 == 0 else "ac",
                                                extra_waits=[("dg", dg_n)])
                    wsslot_user[slot] = ("pe", pe_n)

            emit_attn_w()

            def fvts(g, n=ffn1_dv, ng=cnt["gp"]):
                w(g, "dv", n)
                w(g, "gp", ng)
                g.memset(bass.AP(vts, 64,
                                 [[2 * 520, 128], [520, 2], [65, 8]]), 1.0).then_inc(S["gp"], 1)
                cnt["gp"] += 1
            gp(fvts)
            vts_gp = cnt["gp"]

            st = {"x1t": {}, "proj": {}, "projqk": {}, "qkc": {}, "fmsq": {}, "fm_dv": {},
                  "fm_a": {}, "phi": {}, "psi": {}, "vev": {}, "tr": {},
                  "spe": {}, "mask": {}, "ope": {}, "oev": {}, "upd": {},
                  "sc": {}, "odma": {}, "oag": {}}

            def x1t_load(c):
                rank, h, tloc = c // 16, (c % 16) // 8, c % 8

                def fy(y, c=c, rank=rank, h=h, tloc=tloc):
                    w(y, "cc", x1_ag - 1 + h)
                    if c - 3 >= 0:
                        w(y, "pe", st["proj"][c - 3])
                    y.dma_start(
                        out=x1t[:, (c % 3) * 1024:((c % 3) + 1) * 1024],
                        in_=bass.AP(cc_x1_outh[h].tensor,
                                    rank * D * 1024 + tloc * 128,
                                    [[1024, 128], [128 * 1024, 8], [1, 128]]),
                    ).then_inc(S["dy"], 16)
                    cnt["dy"] += 16
                sy(fy)
                st["x1t"][c] = cnt["dy"]

            def proj_qk(c, extra=None):
                def ft(t, c=c, extra=extra):
                    w(t, "dy", st["x1t"][c])
                    if extra is not None:
                        w(t, extra[0], extra[1])
                    if c >= 1:
                        w(t, "dv", st["fm_a"][c - 1])
                    for dc in range(8):
                        lhs = x1t[:, (c % 3) * 1024 + dc * 128:
                                  (c % 3) * 1024 + (dc + 1) * 128]
                        t.matmul(p_qk[:, 0:128], lhs, wqT[:, dc * 128:(dc + 1) * 128],
                                 start=(dc == 0), stop=False)
                        li = t.matmul(p_qk[:, 128:256], lhs,
                                      wkT[:, dc * 128:(dc + 1) * 128],
                                      start=False, stop=(dc == 7))
                    li.then_inc(S["pe"], 1)
                    cnt["pe"] += 1
                pe(ft)
                st["projqk"][c] = cnt["pe"]

            def proj_v(c):
                def ft(t, c=c):
                    w(t, "dy", st["x1t"][c])
                    if c >= 1:
                        w(t, "dv", st["mask"][(c - 1, 1)])
                    for dc in range(8):
                        lhs = x1t[:, (c % 3) * 1024 + dc * 128:
                                  (c % 3) * 1024 + (dc + 1) * 128]
                        li = t.matmul(p_vs[:], lhs, wvT[:, dc * 512:(dc + 1) * 512],
                                      start=(dc == 0), stop=(dc == 7))
                    li.then_inc(S["pe"], 1)
                    cnt["pe"] += 1
                pe(ft)
                st["proj"][c] = cnt["pe"]

            def fm_a(c):
                def fvA(v, c=c):
                    w(v, "pe", st["projqk"][c])
                    v.tensor_copy(qkbuf[:], p_qk[:, 0:256])
                    qk3 = qkbuf[:].rearrange("p (h a) -> p h a", h=16)
                    v.reduce_sum(fmst[:, 0:16], qk3, axis=AXL.X)
                    v.tensor_mul(sq_scr[:, 0:256], qkbuf[:], qkbuf[:])
                    v.reduce_sum(fmst[:, 16:32],
                                 sq_scr[:, 0:256].rearrange("p (h a) -> p h a", h=16),
                                 axis=AXL.X)
                    mu, var = fmst[:, 32:48], fmst[:, 48:64]
                    v.drain()
                    v.tensor_scalar(mu, fmst[:, 0:16], 1.0 / FD, 0.0, AX.mult, AX.add)
                    v.tensor_scalar(var, fmst[:, 16:32], 1.0 / FD, 0.0, AX.mult, AX.add)
                    v.drain()
                    v.tensor_mul(fmst[:, 64:80], mu, mu)
                    v.drain()
                    v.tensor_sub(var, var, fmst[:, 64:80])
                    v.drain().then_inc(S["dv"], 1)
                    cnt["dv"] += 1
                dv(fvA)
                st["fm_a"][c] = cnt["dv"]

            def fm_b(c):
                def fa2(s_, n=st["fm_a"][c]):
                    w(s_, "dv", n)
                    s_.activation(fmst[:, 48:64], fmst[:, 48:64], AF.Sqrt,
                                  bias=EPS, scale=1.0)
                    s_.drain().then_inc(S["ac"], 1)
                    cnt["ac"] += 1
                ac(fa2)

                def fv3(v, c=c, n=cnt["ac"]):
                    w(v, "ac", n)
                    inv = fmst[:, 80:96]
                    v.reciprocal(inv, fmst[:, 48:64])
                    v.drain()
                    slot = (c % 2) * 128
                    for which, (go, bo, dstb) in enumerate(
                            ((0, FD, qfm), (2 * FD, 3 * FD, kfm))):
                        mu_b = bass.AP(fmst, 32 + which * 8, [[96, 128], [1, 8], [0, 16]])
                        inv_b = bass.AP(fmst, 80 + which * 8, [[96, 128], [1, 8], [0, 16]])
                        g_b = bass.AP(gb, go, [[4 * FD, 128], [0, 8], [1, 16]])
                        b_b = bass.AP(gb, bo, [[4 * FD, 128], [0, 8], [1, 16]])
                        dst = sq_scr[:, 512 + which * 128: 512 + (which + 1) * 128]
                        d3 = dst.rearrange("p (h a) -> p h a", h=8)
                        v.tensor_sub(d3, qkbuf[:, which * 128:(which + 1) * 128]
                                     .rearrange("p (h a) -> p h a", h=8), mu_b)
                        v.tensor_mul(d3, d3, inv_b)
                        v.tensor_mul(d3, d3, g_b)
                        li = v.tensor_add(dstb[:, slot:slot + 128]
                                          .rearrange("p (h a) -> p h a", h=8), d3, b_b)
                    li.then_inc(S["dv"], 1)
                    cnt["dv"] += 1
                dv(fv3)
                st["fm_dv"][c] = cnt["dv"]

            def vevac(c):
                def fv(v, c=c):
                    w(v, "pe", st["proj"][c])
                    dst = bass.AP(vts, (c % 2) * 520, [[2 * 520, 128], [65, 8], [1, 64]])
                    v.tensor_copy(dst, p_vs[:].rearrange("p (h d) -> p h d",
                                                         h=8)).then_inc(S["dv"], 1)
                    cnt["dv"] += 1
                dv(fv)
                st["vev"][c] = cnt["dv"]

            def construct(c):
                slot = c % 2

                def fv(v, c=c, slot=slot):
                    w(v, "dv", st["fm_dv"][c])
                    in0 = bass.AP(qfm, slot * 128, [[256, 128], [16, 8], [1, 16], [0, 16]])
                    in1 = bass.AP(qfm, slot * 128, [[256, 128], [16, 8], [0, 16], [1, 16]])
                    v.tensor_tensor(
                        bass.AP(phi, slot * 2048,
                                [[2 * 2048, 128], [256, 8], [16, 16], [1, 16]]),
                        in0, in1, AX.mult).then_inc(S["dv"], 1)
                    cnt["dv"] += 1
                dv(fv)
                st["phi"][c] = cnt["dv"]

                def fg(v, c=c, slot=slot):
                    in0 = bass.AP(kfm, slot * 128, [[256, 128], [16, 8], [1, 16], [0, 16]])
                    in1 = bass.AP(kfm, slot * 128, [[256, 128], [16, 8], [0, 16], [1, 16]])
                    v.tensor_tensor(
                        bass.AP(psi, slot * 2048,
                                [[2 * 2048, 128], [256, 8], [16, 16], [1, 16]]),
                        in0, in1, AX.mult).then_inc(S["dv"], 1)
                    cnt["dv"] += 1
                dv(fg)
                st["psi"][c] = cnt["dv"]

            def transposes(c):
                slot = c % 2
                srcs = [bass.AP(phi, slot * 2048 + k * 128, [[2 * 2048, 128], [1, 128]])
                        for k in range(16)]
                srcs += [bass.AP(psi, slot * 2048 + k * 128, [[2 * 2048, 128], [1, 128]])
                         for k in range(16)]
                dsts = [(phiT, k * 128) for k in range(16)] + \
                       [(psiT, k * 128) for k in range(16)]
                for gi in range(4):
                    items = [(srcs[gi * 8 + k], dsts[gi * 8 + k][0], dsts[gi * 8 + k][1])
                             for k in range(8)]
                    ew = []
                    if gi == 0:
                        ew = [("dv", st["psi"][c])]
                    dstb = (phiT, psiT)[gi // 2]
                    boff = (gi % 2) * 1024
                    tr_group(items, "A" if gi % 2 == 0 else "B",
                             "dv" if gi % 2 == 0 else "ac", extra_waits=ew,
                             batch_evac=dstb[:, boff:boff + 1024])
                st["tr"][c] = cnt["pe"]
                st["trlast"] = (kk.trfree["A"], kk.trfree["B"])

            def srounds(c, r):
                def ft(t, c=c, r=r):
                    if r == 0:
                        w(t, kk.trfree["A"][0], kk.trfree["A"][1])
                        w(t, kk.trfree["B"][0], kk.trfree["B"][1])
                        w(t, "dv", st["vev"][c])
                    else:
                        w(t, "dv", st["mask"][(c, 0)])
                    for hh in range(4):
                        h = r * 4 + hh
                        for half in range(2):
                            ti = h * 2 + half
                            li = t.matmul(p_vs[:, hh * 128:(hh + 1) * 128],
                                          psiT[:, ti * 128:(ti + 1) * 128],
                                          phiT[:, ti * 128:(ti + 1) * 128],
                                          start=(hh == 0 and half == 0),
                                          stop=(hh == 3 and half == 1))
                    li.then_inc(S["pe"], 1)
                    cnt["pe"] += 1
                pe(ft)
                st["spe"][(c, r)] = cnt["pe"]

            def mask(c, r):
                def fv(v, c=c, r=r):
                    w(v, "pe", st["spe"][(c, r)])
                    w(v, "gp", const_gp)
                    tri_b = bass.AP(trimask, 0, [[128, 128], [0, 4], [1, 128]])
                    v.tensor_tensor(
                        s2m[:, r * 512:(r + 1) * 512].rearrange("p (h i) -> p h i", h=4),
                        p_vs[:].rearrange("p (h i) -> p h i", h=4),
                        tri_b, AX.mult).then_inc(S["dv"], 1)
                    cnt["dv"] += 1
                dv(fv)
                st["mask"][(c, r)] = cnt["dv"]

            def oround(c, r):
                def ft(t, c=c, r=r):
                    if c == 0:
                        w(t, "gp", vts_gp)
                    w(t, "dv", st["mask"][(c, r)])
                    if c >= 1:
                        w(t, "dv", st["sc"][c - 1])
                    if r == 1:
                        w(t, "dv", st["oev"][(c, 0)])
                    elif c >= 1:
                        w(t, "dv", st["oev"][(c - 1, 1)])
                    for hh in range(4):
                        h = r * 4 + hh
                        vsl = bass.AP(vts, (c % 2) * 520 + h * 65, [[2 * 520, 128], [1, 65]])
                        li = t.matmul(p_o[:, hh * 65:(hh + 1) * 65],
                                      s2m[:, (r * 4 + hh) * 128:(r * 4 + hh + 1) * 128],
                                      vsl, start=(hh == 0), stop=(c == 0 and hh == 3))
                        if c >= 1:
                            for half in range(2):
                                ti = h * 2 + half
                                sb_, sl = ti // 7, ti % 7
                                li = t.matmul(p_o[:, hh * 65:(hh + 1) * 65],
                                              phiT[:, ti * 128:(ti + 1) * 128],
                                              bass.AP(Ssb, sb_ * 512 + sl * 65,
                                                      [[1536, 128], [1, 65]]),
                                              start=False, stop=(hh == 3 and half == 1))
                    li.then_inc(S["pe"], 1)
                    cnt["pe"] += 1
                pe(ft)
                st["ope"][(c, r)] = cnt["pe"]

            def upd(c, r):
                def ft(t, c=c, r=r):
                    w(t, "dv", st["psi"][c])
                    for hh in range(4):
                        h = r * 4 + hh
                        vsl = bass.AP(vts, (c % 2) * 520 + h * 65, [[2 * 520, 128], [1, 65]])
                        for half in range(2):
                            ti = h * 2 + half
                            sb_, sl = ti // 7, ti % 7
                            li = t.matmul(bass.AP(p_S[sb_], sl * 65, [[512, 128], [1, 65]]),
                                          bass.AP(psi, (c % 2) * 2048 + ti * 128,
                                                  [[2 * 2048, 128], [1, 128]]),
                                          vsl,
                                          start=(c == 0 and sl == 0),
                                          stop=(c == NCH - 1 and ti in (6, 13, 15)),
                                          skip_group_check=True)
                    li.then_inc(S["pe"], 1)
                    cnt["pe"] += 1
                pe(ft)
                st["upd"][(c, r)] = cnt["pe"]

            def o_evac(c, r):
                def fv(v, c=c, r=r):
                    w(v, "pe", st["ope"][(c, r)])
                    if c >= 2 and r == 0:
                        w(v, "do", st["odma"][c - 2])
                    zv = bass.AP(p_o, 64, [[512, 128], [65, 4]])
                    v.tensor_scalar(zbuf[:, 0:4], zv, 1.0, EPS, AX.mult, AX.add)
                    v.drain()
                    v.reciprocal(zbuf[:, 4:8], zbuf[:, 0:4])
                    v.drain()
                    ov = bass.AP(p_o, 0, [[512, 128], [65, 4], [1, 64]])
                    zr_b = bass.AP(zbuf, 4, [[16, 128], [1, 4], [0, 64]])
                    v.tensor_tensor(
                        bass.AP(oev, (c % 2) * 512 + r * 256,
                                [[2 * 512, 128], [64, 4], [1, 64]]),
                        ov, zr_b, AX.mult).then_inc(S["dv"], 1)
                    cnt["dv"] += 1
                dv(fv)
                st["oev"][(c, r)] = cnt["dv"]

            def snapshot(c):
                def fv(v, c=c):
                    w(v, "pe", st["upd"][(c, 1)])
                    for i, ncol in ((0, 455), (1, 455), (2, 130)):
                        li = v.tensor_copy(Ssb[:, i * 512:i * 512 + ncol],
                                           p_S[i][:, 0:ncol])
                    li.then_inc(S["dv"], 1)
                    cnt["dv"] += 1
                dv(fv)
                st["sc"][c] = cnt["dv"]

            def o_dma(c):
                def fy(y, c=c):
                    w(y, "dv", st["oev"][(c, 1)])
                    y.dma_start(
                        out=bass.AP(cc_o_in.tensor, c * 128 * 512, [[512, 128], [1, 512]]),
                        in_=oev[:, (c % 2) * 512:((c % 2) + 1) * 512],
                    ).then_inc(S["do"], 16)
                    cnt["do"] += 16
                sy(fy)
                st["odma"][c] = cnt["do"]

            if PHASES == "x1t3":
                for c in range(3):
                    x1t_load(c)

                def fxp(g, ndy=cnt["dy"]):
                    w(g, "dy", ndy)
                    g.dma_start(
                        out=bass.AP(y_out.tensor, 0, [[16384, 128], [1, 3072]]),
                        in_=x1t[:, 0:3072]).then_inc(S["dg"], 16)
                    cnt["dg"] += 16
                gp(fxp)

                def ffxp(y, n=cnt["do"], nd=cnt["dg"]):
                    y.wait_ge(S["do"], n)
                    y.wait_ge(S["dg"], nd)
                sy(ffxp)
                attn_es.close()
                return nc

            for c in range(3):
                x1t_load(c)
            proj_qk(0, extra=("dv", ffn1_dv))
            proj_v(0)
            fm_a(0)
            fm_b(0)
            vevac(0)
            construct(0)
            for c in range(1, NCH + 1):
                cp = c - 1
                if c < NCH:
                    proj_qk(c)
                transposes(cp)
                srounds(cp, 0)
                mask(cp, 0)
                if c < NCH:
                    fm_a(c)
                srounds(cp, 1)
                mask(cp, 1)
                oround(cp, 0)
                upd(cp, 0)
                o_evac(cp, 0)
                if c < NCH:
                    fm_b(c)
                    construct(c)
                if c < NCH:
                    proj_v(c)
                    if c + 2 < NCH and (c + 2) not in st["x1t"]:
                        x1t_load(c + 2)
                oround(cp, 1)
                upd(cp, 1)
                o_evac(cp, 1)
                snapshot(cp)
                o_dma(cp)
                if PHASES == "a0" and c == 1:
                    def fa0(g, nd=cnt["dv"], na=cnt["ac"], np_=cnt["pe"],
                            ng=cnt["gp"], ndo=cnt["do"], ndy=cnt["dy"]):
                        w(g, "dv", nd)
                        w(g, "ac", na)
                        w(g, "pe", np_)
                        w(g, "gp", ng)
                        w(g, "do", ndo)
                        w(g, "dy", ndy)
                        dumps = [(x1t, 0, 1024), (qfm, 0, 128), (kfm, 0, 128),
                                 (phi, 0, 2048), (psi, 0, 2048), (s2m, 0, 1024),
                                 (vts, 0, 520), (oev, 0, 512), (zbuf, 0, 16),
                                 (phiT, 0, 2048), (psiT, 0, 2048)]
                        off = 0
                        for buf, so, ncol in dumps:
                            g.dma_start(
                                out=bass.AP(y_out.tensor, off,
                                            [[16384, 128], [1, ncol]]),
                                in_=buf[:, so:so + ncol]).then_inc(S["dg"], 16)
                            cnt["dg"] += 16
                            off += ncol
                    gp(fa0)

                    def ffa0(y, n=cnt["do"], nd=cnt["dg"]):
                        y.wait_ge(S["do"], n)
                        y.wait_ge(S["dg"], nd)
                    sy(ffa0)
                    attn_es.close()
                    return nc
                if c < NCH:
                    vevac(c)
                if cp % 8 == 7:
                    q = cp // 8

                    def fcc2(g, q=q, n=st["odma"][cp]):
                        w(g, "do", n)
                        g.collective_compute(
                            "AllGather", AX.bypass, replica_groups=GROUPS,
                            ins=[bass.AP(cc_o_in.tensor, q * 1024 * 512,
                                         [[512, 1024], [1, 512]])],
                            outs=[cc_o_out[q]]).then_inc(S["cc"], 1)
                        cnt["cc"] += 1
                    gp(fcc2)
                    st["oag"][q] = cnt["cc"]

            for c in range(NCH):
                assert c in st["x1t"], f"x1t chunk {c} never loaded"

            if PHASES == "odump":
                def fod(g, n=st["odma"][NCH - 1]):
                    w(g, "do", n)
                    g.dma_start(
                        out=bass.AP(y_out.tensor, 0, [[512, 4096], [1, 512]]),
                        in_=bass.AP(cc_o_in.tensor, 0, [[512, 4096], [1, 512]]),
                    ).then_inc(S["dg"], 16)
                    cnt["dg"] += 16
                gp(fod)

                def ffo(y, n=cnt["do"], nd=cnt["dg"]):
                    y.wait_ge(S["do"], n)
                    y.wait_ge(S["dg"], nd)
                sy(ffo)
                attn_es.close()
                return nc

            scan_pe = cnt["pe"]
            wo_pe = {}
            parv = nc.axis_index(bass.AxisInfo(2, 1))
            pconds = [parv == 0, parv == 1]

            def fpar(g):
                g.compute_val(pconds[0])
                g.compute_val(pconds[1])
            gp(fpar)
            for qi in range(2):
                def fga(g, qi=qi):
                    for pv in range(2):
                        q = pv * 2 + qi
                        w(g, "cc", st["oag"][q])
                        if qi == 1:
                            w(g, "pe", wo_pe[7])
                        for rank in range(2):
                            g.dma_start(
                                out=bass.AP(o_asm8, rank * 512,
                                            [[8192, 128], [1024, 8], [1, 512]]),
                                in_=bass.AP(cc_o_out[q].tensor, rank * 1024 * 512,
                                            [[512, 128], [128 * 512, 8], [1, 512]]),
                                cond=pconds[pv],
                            ).then_inc(S["dg"], 16)
                            cnt["dg"] += 16
                gp(fga)
                asm_dg = cnt["dg"]

                for tl in range(8):
                    t_ = qi * 8 + tl
                    items = [(o_asm8[:, tl * 1024 + ofc * 128:
                                     tl * 1024 + (ofc + 1) * 128], oT, ofc * 128)
                             for ofc in range(8)]
                    ew = [("dg", asm_dg)]
                    if t_ == 0:
                        ew.append(("pe", scan_pe))
                        ew.append(("dv", st["sc"][NCH - 1]))
                    (ek, en), _pe_n = tr_group(items, "A" if t_ % 2 == 0 else "B",
                                               "dv" if t_ % 2 == 0 else "ac",
                                               extra_waits=ew, batch_evac=oT[:])

                    def ft2(t, t_=t_, ek=ek, en=en,
                            pf=0 if t_ == 0 else None):
                        w(t, ek, en)
                        if t_ >= 1:
                            w(t, "dv", x2_upd.get(t_ - 1, 0))
                        for ofc in range(8):
                            lhs = oT[:, ofc * 128:(ofc + 1) * 128]
                            for dh in range(2):
                                li = t.matmul(p_mm[2 + dh][:], lhs,
                                              woT[:, ofc * 1024 + dh * 512:
                                                  ofc * 1024 + (dh + 1) * 512],
                                              start=(ofc == 0), stop=(ofc == 7))
                        li.then_inc(S["pe"], 1)
                        cnt["pe"] += 1
                    pe(ft2)
                    wo_pe[t_] = cnt["pe"]

                    def fvr(v, t_=t_, n=cnt["pe"]):
                        w(v, "pe", n)
                        for dh in range(2):
                            xsl = x_res[:, t_ * D + dh * 512: t_ * D + (dh + 1) * 512]
                            li = v.tensor_add(xsl, xsl, p_mm[2 + dh][:])
                        li.then_inc(S["dv"], 1)
                        cnt["dv"] += 1
                    dv(fvr)
                    x2_upd[t_] = cnt["dv"]

            attn_es.close()
            ffn2_mm_free = ("dv", cnt["dv"])
        else:
            ffn2_mm_free = ("dv", ffn1_dv)

        if PHASES == "attno":
            def fdat(g, nd=cnt["dv"], na=cnt["ac"]):
                w(g, "dv", nd)
                w(g, "ac", na)
                g.dma_start(
                    out=bass.AP(y_out.tensor, 0, [[D, 128], [128 * D, NCHO], [1, D]]),
                    in_=bass.AP(x_res, 0, [[NCHO * D, 128], [D, NCHO], [1, D]]),
                ).then_inc(S["dg"], 16)
                cnt["dg"] += 16
            gp(fdat)

            def ffat(y, n=cnt["do"], nd=cnt["dg"]):
                y.wait_ge(S["do"], n)
                y.wait_ge(S["dg"], nd)
            sy(ffat)
            return nc

        ffn_es2 = ExitStack()
        bufs2 = {
            "Hh": sb(ffn_es2, "Hh2", [128, NJ * 1024]),
            "wgT_b": sb(ffn_es2, "wgT2", [128, 3 * 1024]),
            "wuT_b": sb(ffn_es2, "wuT2", [128, 3 * 1024]),
            "wdT_f": sb(ffn_es2, "wdT2", [128, NJ * 1024]),
        }
        emit_ffn(2, bufs2, lambda tg: ("dv", x2_upd[tg]), ffn2_mm_free)
        ffn_es2.close()

        def ffin(y, n=cnt["do"]):
            y.wait_ge(S["do"], n)
        sy(ffin)

    return nc


def kernel(L0_attn_x, L0_attn_Wq, L0_attn_Wk, L0_attn_Wv, L0_attn_Wo,
           L0_attn_gamma, L0_attn_beta, L0_ffn_w_gate, L0_ffn_w_up, L0_ffn_w_down):
    x = np.ascontiguousarray(np.asarray(L0_attn_x, dtype=np.float32))
    wq = np.ascontiguousarray(np.asarray(L0_attn_Wq, dtype=np.float32))
    wk = np.ascontiguousarray(np.asarray(L0_attn_Wk, dtype=np.float32))
    wv = np.ascontiguousarray(np.asarray(L0_attn_Wv, dtype=np.float32))
    wo = np.ascontiguousarray(np.asarray(L0_attn_Wo, dtype=np.float32))
    gam = np.ascontiguousarray(np.asarray(L0_attn_gamma, dtype=np.float32))
    bet = np.ascontiguousarray(np.asarray(L0_attn_beta, dtype=np.float32))
    wg = np.ascontiguousarray(np.asarray(L0_ffn_w_gate, dtype=np.int32))
    wu = np.ascontiguousarray(np.asarray(L0_ffn_w_up, dtype=np.int32))
    wd = np.ascontiguousarray(np.asarray(L0_ffn_w_down, dtype=np.int32))

    nc = build_nc()
    in_maps = []
    for c in range(8):
        b, p = c // 2, c % 2
        in_maps.append({
            "x_own": np.ascontiguousarray(x[b, p * TOK:(p + 1) * TOK, :]),
            "wq_own": np.ascontiguousarray(wq[p * 128:(p + 1) * 128, :]),
            "wk_own": np.ascontiguousarray(wk[p * 128:(p + 1) * 128, :]),
            "wv_own": np.ascontiguousarray(wv[p * 512:(p + 1) * 512, :]),
            "wo": wo, "gamma": gam, "beta": bet,
            "w_gate": wg, "w_up": wu, "w_down": wd,
        })
    res = run_bass_kernel_spmd(nc, in_maps, core_ids=list(range(8)))
    out = np.empty((B, L, D), dtype=np.float32)
    for c in range(8):
        b, p = c // 2, c % 2
        out[b, p * TOK:(p + 1) * TOK, :] = res.results[c]["y_own"]
    return out


# version 10
# speedup vs baseline: 1.0057x; 1.0057x over previous
import numpy as np
from contextlib import ExitStack

import concourse.bass as bass
import concourse.mybir as mybir
from concourse.bass_utils import run_bass_kernel_spmd

F32 = mybir.dt.float32
BF16 = mybir.dt.bfloat16
I32 = mybir.dt.int32
AX = mybir.AluOpType
AF = mybir.ActivationFunctionType
AXL = mybir.AxisListType

B, L, D = 4, 4096, 1024
HID = 3072
FD, HD = 16, 64
TOK = 2048
NCHO = 16
NCH = 32
NJ = HID // 128
EPS = 1e-5
ZK = 1.0 / 256.0
QSC = FD ** -0.5
FSC = 0.5 / (127.0 * 127.0)

SKIP_ATTN = False
PHASES = "all"
LNCH = 8
GROUPS = [[0, 1], [2, 3], [4, 5], [6, 7]]


class K:

    def __init__(self, nc, block, S):
        self.nc, self.block, self.S = nc, block, S
        self.cnt = {k: 0 for k in S}
        self.trfree = {}

    def inc(self, inst, key, k=1):
        inst.then_inc(self.S[key], k)
        self.cnt[key] += k
        return self.cnt[key]

    def bump(self, key, k=1):
        self.cnt[key] += k
        return self.cnt[key]

    @staticmethod
    def w(e, S, key, n):
        if n > 0:
            e.wait_ge(S[key], n)


def build_nc():
    nc = bass.Bass("TRN2", target_bir_lowering=False, detect_race_conditions=False)

    x_in = nc.dram_tensor("x_own", [TOK, D], F32, kind="ExternalInput").ap()
    wq_n = nc.dram_tensor("wq_own", [128, D], F32, kind="ExternalInput").ap()
    wk_n = nc.dram_tensor("wk_own", [128, D], F32, kind="ExternalInput").ap()
    wv_n = nc.dram_tensor("wv_own", [512, D], F32, kind="ExternalInput").ap()
    wo_n = nc.dram_tensor("wo", [D, D], F32, kind="ExternalInput").ap()
    gam_in = nc.dram_tensor("gamma", [FD], F32, kind="ExternalInput").ap()
    bet_in = nc.dram_tensor("beta", [FD], F32, kind="ExternalInput").ap()
    wg_n = nc.dram_tensor("w_gate", [HID, D], I32, kind="ExternalInput").ap()
    wu_n = nc.dram_tensor("w_up", [HID, D], I32, kind="ExternalInput").ap()
    wd_n = nc.dram_tensor("w_down", [D, HID], I32, kind="ExternalInput").ap()
    y_out = nc.dram_tensor("y_own", [TOK, D], F32, kind="ExternalOutput").ap()

    wgT_sc = nc.dram_tensor("wgT_sc", [NJ, 8, 128, 128], BF16).ap()
    wuT_sc = nc.dram_tensor("wuT_sc", [NJ, 8, 128, 128], BF16).ap()
    wdT_sc = nc.dram_tensor("wdT_sc", [NJ, 8, 128, 128], BF16).ap()
    cc_x1_in = nc.dram_tensor("cc_x1_in", [2, D, 1024], BF16).ap()
    cc_x1_outh = [nc.dram_tensor(f"cc_x1_out{h}", [2 * D, 1024], BF16).ap()
                  for h in range(2)]
    cc_o_in = nc.dram_tensor("cc_o_in", [L, 512], BF16).ap()
    cc_o_out = [nc.dram_tensor(f"cc_o_out{q}", [2048, 512], BF16).ap() for q in range(4)]

    es = ExitStack()

    def sb(e, name, shape, dt=BF16):
        return e.enter_context(nc.sbuf_tensor(name, shape, dt))

    with es:
        x_res = sb(es, "x_res", [128, NCHO * D])
        xT = sb(es, "xT", [128, 8 * TOK])
        xnbuf = sb(es, "xnbuf", [128, 2 * 1024])
        wstage = sb(es, "wstage", [128, 6 * 1024])
        sq_scr = sb(es, "sq_scr", [128, 1024], F32)
        lnsq = sb(es, "lnsq", [128, 1024])
        lnstat = sb(es, "lnstat", [128, 128], F32)
        outst = sb(es, "outst", [128, 2 * 1024], F32)
        oev = sb(es, "oev", [128, 2 * 512])
        ident = sb(es, "ident", [128, 128])
        trimask = sb(es, "trimask", [128, 128])
        gb = sb(es, "gb", [128, 4 * FD], F32)
        gbr = sb(es, "gbr", [128, 2 * FD], F32)
        zbuf = sb(es, "zbuf", [128, 16], F32)
        fmst = sb(es, "fmst", [128, 96], F32)
        qkbuf = sb(es, "qkbuf", [128, 256], F32)
        cst = sb(es, "cst", [128, 4], F32)

        p_trA = es.enter_context(nc.psum_tensor("p_trA", [128, 1024], BF16))
        p_trB = es.enter_context(nc.psum_tensor("p_trB", [128, 1024], BF16))
        p_mm = [es.enter_context(nc.psum_tensor(f"p_mm{i}", [128, 512], F32))
                for i in range(6)]
        p_qk, p_vs = p_mm[0], p_mm[1]
        p_S = [p_mm[2], p_mm[3], p_mm[4]]
        p_o = p_mm[5]

        semnames = ["pe", "dv", "ac", "gp", "dy", "dg", "do", "cc"]
        S = {k: es.enter_context(nc.semaphore("s_" + k)) for k in semnames}
        block = es.enter_context(nc.Block())
        nc.cache_partition_id()
        for ci, cv in enumerate((EPS, 1.0 / 127.0, 0.0, 1.0)):
            nc.const_aps.register(F32, cv, cst[:, ci:ci + 1])
        kk = K(nc, block, S)
        cnt = kk.cnt

        constgp = [0]

        def w(e, key, n):
            if n > 0:
                e.wait_ge(S[key], n)

        def pe(f): block.tensor(f)
        def dv(f): block.vector(f)
        def ac(f): block.scalar(f)
        def gp(f): block.gpsimd(f)
        def sy(f): block.sync(f)

        def tr_group(items, bank, evac_eng, extra_waits=(), batch_evac=None):
            pbank = p_trA if bank == "A" else p_trB
            free = kk.trfree.get(bank)
            waits = list(extra_waits)
            if free is not None:
                waits.append(free)

            def ft(t, items=items, pbank=pbank, waits=tuple(waits)):
                w(t, "gp", constgp[0])
                for key, n in waits:
                    w(t, key, n)
                li = None
                for i, (src, _d, _o) in enumerate(items):
                    li = t.transpose(pbank[:, i * 128:(i + 1) * 128], src, ident[:])
                li.then_inc(S["pe"], 1)
                cnt["pe"] += 1
            pe(ft)
            pe_n = cnt["pe"]

            if PHASES == "lnt":
                res = ("dv", 0)
                kk.trfree[bank] = ("pe", pe_n)
                return res, pe_n
            if evac_eng == "dv":
                def fe(v, items=items, pbank=pbank, n=pe_n, batch=batch_evac):
                    w(v, "pe", n)
                    if batch is not None:
                        li = v.tensor_copy(batch, pbank[:, 0:128 * len(items)])
                    else:
                        for i, (_s, dstb, off) in enumerate(items):
                            li = v.tensor_copy(dstb[:, off:off + 128],
                                               pbank[:, i * 128:(i + 1) * 128])
                    li.then_inc(S["dv"], 1)
                    cnt["dv"] += 1
                dv(fe)
                res = ("dv", cnt["dv"])
            else:
                def fe(s, items=items, pbank=pbank, n=pe_n, batch=batch_evac):
                    w(s, "pe", n)
                    if batch is not None:
                        li = s.copy(batch, pbank[:, 0:128 * len(items)])
                    else:
                        for i, (_s2, dstb, off) in enumerate(items):
                            li = s.copy(dstb[:, off:off + 128],
                                        pbank[:, i * 128:(i + 1) * 128])
                    li.then_inc(S["ac"], 1)
                    cnt["ac"] += 1
                ac(fe)
                res = ("ac", cnt["ac"])
            kk.trfree[bank] = res
            return res, pe_n

        def f00(v):
            for ci, cv in enumerate((EPS, 1.0 / 127.0, 0.0, 1.0)):
                v.memset(cst[:, ci:ci + 1], cv)
            if PHASES == "lnh0":
                v.memset(lnstat[:], 777.0)
            v.memset(ident[:], 1.0)
            v.memset(trimask[:], 1.0).then_inc(S["dv"], 1)
            cnt["dv"] += 1
        dv(f00)
        pre_dv = cnt["dv"]

        def f0(g):
            g.dma_start(
                out=bass.AP(x_res, 0, [[NCHO * D, 128], [D, NCHO], [1, D]]),
                in_=bass.AP(x_in.tensor, 0, [[D, 128], [128 * D, NCHO], [1, D]]),
                ).then_inc(S["dg"], 16)
            cnt["dg"] += 16
            g.dma_start(out=bass.AP(gbr, 0, [[2 * FD, 128], [1, FD]]),
                        in_=bass.AP(gam_in.tensor, 0, [[0, 128], [1, FD]])).then_inc(S["dg"], 16)
            cnt["dg"] += 16
            g.dma_start(out=bass.AP(gbr, FD, [[2 * FD, 128], [1, FD]]),
                        in_=bass.AP(bet_in.tensor, 0, [[0, 128], [1, FD]])).then_inc(S["dg"], 16)
            cnt["dg"] += 16
            w(g, "dv", pre_dv)
            g.affine_select(out=ident[:], in_=ident[:], pattern=[[-1, 128]],
                            compare_op=AX.is_equal, fill=0.0, base=0,
                            channel_multiplier=1).then_inc(S["gp"], 1)
            cnt["gp"] += 1
            g.wait_ge(S["gp"], cnt["gp"])
            g.affine_select(out=trimask[:], in_=trimask[:], pattern=[[1, 128]],
                            compare_op=AX.is_ge, fill=0.0, base=0,
                            channel_multiplier=-1).then_inc(S["gp"], 1)
            cnt["gp"] += 1
        gp(f0)
        x_dg = 16
        gb_dg = 48
        const_gp = cnt["gp"]
        constgp[0] = const_gp

        def f0v(v):
            w(v, "dg", gb_dg)
            v.tensor_scalar(gb[:, 0:FD], gbr[:, 0:FD], QSC, 0.0, AX.mult, AX.add)
            v.tensor_scalar(gb[:, FD:2 * FD], gbr[:, FD:2 * FD], QSC, 0.0, AX.mult, AX.add)
            v.tensor_copy(gb[:, 2 * FD:3 * FD], gbr[:, 0:FD])
            v.tensor_copy(gb[:, 3 * FD:4 * FD], gbr[:, FD:2 * FD]).then_inc(S["dv"], 1)
            cnt["dv"] += 1
        dv(f0v)
        gb_dv = cnt["dv"]

        def emit_ln_half(hf, dep):
            lb = hf * 64
            nch = LNCH if PHASES in ("lnh0", "lnr", "lns", "lna", "lnt") else 8
            for t in range(nch):
                tg = hf * 8 + t
                xc = x_res[:, tg * D:(tg + 1) * D]
                dk, dn = dep(tg)

                def fg(v, t=t, xc=xc, dk=dk, dn=dn):
                    w(v, dk, dn)
                    v.reduce_sum(lnstat[:, lb + t:lb + t + 1], xc, axis=AXL.X)
                    v.tensor_mul(lnsq[:], xc, xc)
                    v.reduce_sum(lnstat[:, lb + 8 + t:lb + 9 + t], lnsq[:],
                                 axis=AXL.X).then_inc(S["dv"], 1)
                    cnt["dv"] += 1
                dv(fg)
            if PHASES == "lnr":
                return ("dv", cnt["dv"]), ("ac", cnt["ac"])

            def fv(v):
                mu, var = lnstat[:, lb + 16:lb + 24], lnstat[:, lb + 24:lb + 32]
                v.drain()
                v.tensor_scalar(mu, lnstat[:, lb + 0:lb + 8], 1.0 / D, 0.0, AX.mult, AX.add)
                v.tensor_scalar(var, lnstat[:, lb + 8:lb + 16], 1.0 / D, 0.0, AX.mult, AX.add)
                v.tensor_scalar(lnstat[:, lb + 56:lb + 64], lnstat[:, lb + 0:lb + 8],
                                -1.0 / D, 0.0, AX.mult, AX.add)
                v.drain()
                v.tensor_mul(lnstat[:, lb + 32:lb + 40], mu, mu)
                v.drain()
                v.tensor_sub(var, var, lnstat[:, lb + 32:lb + 40])
                if PHASES == "lnh0":
                    v.drain()
                    v.tensor_copy(lnstat[:, 80:88], var)
                v.drain().then_inc(S["dv"], 1)
                cnt["dv"] += 1
            dv(fv)

            def fs(s, n=cnt["dv"]):
                w(s, "dv", n)
                if PHASES == "lnh0":
                    s.activation(lnstat[:, 72:80], lnstat[:, lb + 24:lb + 32],
                                 AF.Copy, bias=0.0, scale=1.0)
                s.activation(lnstat[:, lb + 40:lb + 48], lnstat[:, lb + 24:lb + 32], AF.Sqrt,
                             bias=EPS, scale=1.0)
                s.drain().then_inc(S["ac"], 1)
                cnt["ac"] += 1
            ac(fs)

            def fv2(v, n=cnt["ac"]):
                w(v, "ac", n)
                inv = lnstat[:, lb + 48:lb + 56]
                v.reciprocal(inv, lnstat[:, lb + 40:lb + 48])
                v.drain()
                v.tensor_mul(lnstat[:, lb + 32:lb + 40], lnstat[:, lb + 56:lb + 64],
                             inv)
                v.drain().then_inc(S["dv"], 1)
                cnt["dv"] += 1
            dv(fv2)
            inv_dv = cnt["dv"]
            if PHASES == "lns":
                return ("dv", cnt["dv"]), ("ac", cnt["ac"])

            for t in range(nch):
                tg = hf * 8 + t
                xc = x_res[:, tg * D:(tg + 1) * D]
                slot = t % 2

                def fv3(v, xc=xc, t=t, slot=slot, n=inv_dv):
                    w(v, "dv", n)
                    v.tensor_scalar(xnbuf[:, slot * 1024:(slot + 1) * 1024], xc,
                                    lnstat[:, lb + 48 + t:lb + 49 + t], lnstat[:, lb + 32 + t:lb + 33 + t],
                                    AX.mult, AX.add).then_inc(S["dv"], 1)
                    cnt["dv"] += 1
                dv(fv3)
                ap_dv = cnt["dv"]
                if PHASES == "lna":
                    continue

                items = [(xnbuf[:, slot * 1024 + dc * 128: slot * 1024 + (dc + 1) * 128],
                          xT, dc * TOK + hf * 1024 + t * 128) for dc in range(8)]
                tr_group(items, "A" if slot == 0 else "B",
                         "dv" if slot == 0 else "ac",
                         extra_waits=[("dv", ap_dv)])
            return (kk.trfree.get("A", ("dv", 0)), kk.trfree.get("B", ("ac", 0)))

        wsslot_user = {}

        def stage_load(slot, src, cast_queue="dg"):
            prev = wsslot_user.get(slot)

            def fy(g, slot=slot, src=src, prev=prev):
                if prev is not None:
                    w(g, prev[0], prev[1])
                g.dma_start(out=wstage[:, slot * 1024:(slot + 1) * 1024],
                            in_=src).then_inc(S["dg"], 16)
                cnt["dg"] += 16
            gp(fy)
            return cnt["dg"]

        spill_do = {}

        def wslab_transpose(kind, j, dstb, dst_off, spill_sc, slot, busy_wait):
            if kind == "g":
                src = bass.AP(wg_n.tensor, j * 128 * D, [[D, 128], [1, D]])
            elif kind == "u":
                src = bass.AP(wu_n.tensor, j * 128 * D, [[D, 128], [1, D]])
            else:
                src = bass.AP(wd_n.tensor, j * 128, [[HID, 128], [128 * HID, 8], [1, 128]])
            dg_n = stage_load(slot, src)
            stage = wstage[:, slot * 1024:(slot + 1) * 1024]
            items = [(stage[:, dc * 128:(dc + 1) * 128], dstb, dst_off + dc * 128)
                     for dc in range(8)]
            ew = [("dg", dg_n)]
            psp = spill_do.get((kind, dst_off))
            if psp is not None:
                ew.append(("do", psp))
            if busy_wait is not None:
                ew.append(busy_wait)
            (ek, en), pe_n = tr_group(items, "A", "dv", extra_waits=ew,
                                      batch_evac=dstb[:, dst_off:dst_off + 1024])
            wsslot_user[slot] = ("pe", pe_n)

            def fsp(y, dstb=dstb, dst_off=dst_off, spill_sc=spill_sc, j=j, en=en):
                w(y, "dv", en)
                y.dma_start(out=bass.AP(spill_sc.tensor, j * 8 * 128 * 128,
                                        [[128, 128], [128 * 128, 8], [1, 128]]),
                            in_=dstb[:, dst_off:dst_off + 1024]).then_inc(S["do"], 16)
                cnt["do"] += 16
            sy(fsp)
            spill_do[(kind, dst_off)] = cnt["do"]
            return ("dv", en)

        def wslab_stream(sc, j, dstb, dst_off, busy_wait, spill_do):
            def fy(y, sc=sc, j=j, dstb=dstb, dst_off=dst_off,
                   busy_wait=busy_wait, spill_do=spill_do):
                w(y, "do", spill_do)
                if busy_wait is not None:
                    w(y, busy_wait[0], busy_wait[1])
                y.dma_start(out=dstb[:, dst_off:dst_off + 1024],
                            in_=bass.AP(sc.tensor, j * 8 * 128 * 128,
                                        [[128, 128], [128 * 128, 8], [1, 128]]),
                            ).then_inc(S["dy"], 16)
                cnt["dy"] += 16
            sy(fy)
            return ("dy", cnt["dy"])

        spill_done = {"g": 0, "u": 0, "d": 0}

        def emit_ffn(call, bufs, dep, prev_mm_free):
            if PHASES == "p0":
                return {t: cnt["dv"] for t in range(16)}, cnt["dv"]
            """call in {1, 2}; bufs: dict with Hh, wgT_b, wuT_b, wdT_f.
            dep(tg): gate for x_res chunk reads. prev_mm_free: (key, n) that
            must pass before first use of p_mm banks. Returns per-chunk evac
            counts (for call 1: x_res update dv counts)."""
            Hh, wgT_b, wuT_b, wdT_f = bufs["Hh"], bufs["wgT_b"], bufs["wuT_b"], bufs["wdT_f"]
            lnA = emit_ln_half(0, dep)
            if PHASES in ("lnh0", "lnr", "lns", "lna", "lnt"):
                return {t: cnt["dv"] for t in range(16)}, cnt["dv"]
            lnB = emit_ln_half(1, dep)
            ln_dv = cnt["dv"]
            ln_ac = cnt["ac"]
            x_upd = {}
            if PHASES in ("ln1", "lnh0", "lnr", "lns", "lna", "lnt"):
                return {t: cnt["dv"] for t in range(16)}, cnt["dv"]

            gu_mm_pe = {}
            silu_ac = {}
            hmul_dv = {}
            last_down_ev = [0]

            wready = {}

            def emit_tr3(jj):
                busy2 = ("pe", gu_mm_pe[(0, jj - 3)]) if (0, jj - 3) in gu_mm_pe \
                    else None
                wg_r = wslab_transpose("g", jj, wgT_b, (jj % 3) * 1024, wgT_sc,
                                       (3 * jj) % 6, busy2)
                wu_r = wslab_transpose("u", jj, wuT_b, (jj % 3) * 1024,
                                       wuT_sc, (3 * jj + 1) % 6, busy2)
                wd_r = wslab_transpose("d", jj, wdT_f, jj * 1024, wdT_sc,
                                       (3 * jj + 2) % 6, None)
                spill_done["g"] = spill_done["u"] = spill_done["d"] = cnt["do"]
                bufs.setdefault("wd_ready", {})[jj] = wd_r
                wready[jj] = (wg_r, wu_r, wd_r)

            for hf in range(2):
                for j in range(NJ):
                    bslot = j % 3
                    jp3 = (hf, j - 3) if j >= 3 else ((hf - 1, NJ - 3 + j) if hf == 1 else None)
                    busy = ("pe", gu_mm_pe[jp3]) if jp3 in gu_mm_pe else None
                    if call == 2 and busy is None and prev_mm_free is not None:
                        busy = prev_mm_free
                    if call == 1 and hf == 0:
                        if j == 0:
                            for jj in range(3):
                                emit_tr3(jj)
                        wg_r, wu_r, wd_r = wready[j]
                    elif hf == 0:
                        wg_r = wslab_stream(wgT_sc, j, wgT_b, bslot * 1024, busy,
                                            spill_done["g"])
                        wu_r = wslab_stream(wuT_sc, j, wuT_b, bslot * 1024,
                                            busy, spill_done["u"])
                        wd_r = wslab_stream(wdT_sc, j, wdT_f, j * 1024,
                                            prev_mm_free if j == 0 else None,
                                            spill_done["d"])
                        bufs.setdefault("wd_ready", {})[j] = wd_r
                    else:
                        wg_r = wslab_stream(wgT_sc, j, wgT_b, bslot * 1024, busy,
                                            spill_done["g"])
                        wu_r = wslab_stream(wuT_sc, j, wuT_b, bslot * 1024,
                                            busy, spill_done["u"])

                    def ft(t, j=j, hf=hf, bslot=bslot, wg_r=wg_r, wu_r=wu_r,
                           ln_dv=ln_dv, ln_ac=ln_ac, first=(j == 0)):
                        w(t, wg_r[0], wg_r[1])
                        w(t, wu_r[0], wu_r[1])
                        if first:
                            w(t, "dv", ln_dv)
                            w(t, "ac", ln_ac)
                            w(t, "dv", last_down_ev[0])
                            if hf == 0 and prev_mm_free is not None:
                                w(t, prev_mm_free[0], prev_mm_free[1])
                        jp = (hf, j - 1) if j >= 1 else (hf - 1, NJ - 1)
                        if jp in silu_ac:
                            w(t, "ac", silu_ac[jp])
                        if jp in hmul_dv:
                            w(t, "dv", hmul_dv[jp])
                        li = None
                        for which in range(2):
                            wb = wgT_b if which == 0 else wuT_b
                            for dc in range(8):
                                lhs = wb[:, bslot * 1024 + dc * 128:
                                         bslot * 1024 + (dc + 1) * 128]
                                for blk in range(2):
                                    rhs = xT[:, dc * TOK + hf * 1024 + blk * 512:
                                             dc * TOK + hf * 1024 + (blk + 1) * 512]
                                    li = t.matmul(p_mm[2 * which + blk][:], lhs, rhs,
                                                  start=(dc == 0), stop=(dc == 7))
                        li.then_inc(S["pe"], 1)
                        cnt["pe"] += 1
                    pe(ft)
                    gu_mm_pe[(hf, j)] = cnt["pe"]
                    gu_pe = cnt["pe"]
                    if call == 1 and hf == 0 and j + 3 < NJ:
                        emit_tr3(j + 3)

                    def fs(s, n=gu_pe, hp=hmul_dv.get((hf, j - 1) if j >= 1
                                                        else (hf - 1, NJ - 1), 0)):
                        w(s, "pe", n)
                        w(s, "dv", hp)
                        li = None
                        for blk in range(2):
                            li = s.activation(sq_scr[:, blk * 512:(blk + 1) * 512],
                                              p_mm[blk][:], AF.Silu, scale=1.0 / 127.0)
                        li.then_inc(S["ac"], 1)
                        cnt["ac"] += 1
                    ac(fs)
                    silu_ac[(hf, j)] = cnt["ac"]

                    def fv(v, j=j, n_ac=cnt["ac"], n_pe=gu_pe):
                        w(v, "ac", n_ac)
                        w(v, "pe", n_pe)
                        li = None
                        for blk in range(2):
                            li = v.tensor_mul(Hh[:, j * 1024 + blk * 512:
                                                 j * 1024 + (blk + 1) * 512],
                                              sq_scr[:, blk * 512:(blk + 1) * 512],
                                              p_mm[2 + blk][:])
                        li.then_inc(S["dv"], 1)
                        cnt["dv"] += 1
                    dv(fv)
                    hmul_dv[(hf, j)] = cnt["dv"]

                if PHASES in ("hh", "wdt") and hf == 0 and call == 1:
                    def fdh(g, nd=cnt["dv"], na=cnt["ac"], ndo=cnt["do"]):
                        w(g, "dv", nd)
                        w(g, "ac", na)
                        w(g, "do", ndo)
                        src = Hh if PHASES == "hh" else wdT_f
                        v32 = src.bitcast(F32)
                        g.dma_start(
                            out=bass.AP(y_out.tensor, 0, [[12288, 128], [1, 12288]]),
                            in_=v32[:, 0:12288]).then_inc(S["dg"], 16)
                        cnt["dg"] += 16
                    gp(fdh)
                    return {t: cnt["dv"] for t in range(16)}, cnt["dv"]

                down_ev = {}
                for grp in range(4):
                    def ftd(t, hf=hf, grp=grp,
                            h_last=hmul_dv[(hf, NJ - 1)], s_last=silu_ac[(hf, NJ - 1)]):
                        if grp == 0:
                            w(t, "dv", h_last)
                            w(t, "ac", s_last)
                        else:
                            w(t, "dv", down_ev[grp - 1])
                        for j in range(NJ):
                            wd_r = bufs["wd_ready"][j]
                            w(t, wd_r[0], wd_r[1])
                            for ch in range(2):
                                tl = grp * 2 + ch
                                lhs = Hh[:, j * 1024 + tl * 128: j * 1024 + (tl + 1) * 128]
                                for dh in range(2):
                                    rhs = wdT_f[:, j * 1024 + dh * 512:
                                                j * 1024 + (dh + 1) * 512]
                                    li = t.matmul(p_mm[ch * 2 + dh][:], lhs, rhs,
                                                  start=(j == 0), stop=(j == NJ - 1))
                        li.then_inc(S["pe"], 1)
                        cnt["pe"] += 1
                    pe(ftd)
                    d_pe = cnt["pe"]

                    def fvd(v, hf=hf, grp=grp, n=d_pe, call=call):
                        w(v, "pe", n)
                        for ch in range(2):
                            tg = hf * 8 + grp * 2 + ch
                            if call == 2:
                                w(v, "do", outdma.get(tg - 2, 0))
                            for dh in range(2):
                                pmm = p_mm[ch * 2 + dh][:]
                                xsl = x_res[:, tg * D + dh * 512: tg * D + (dh + 1) * 512]
                                if call == 1:
                                    li = v.scalar_tensor_tensor(xsl, pmm, FSC, xsl,
                                                                AX.mult, AX.add)
                                else:
                                    li = v.scalar_tensor_tensor(
                                        outst[:, (tg % 2) * 1024 + dh * 512:
                                              (tg % 2) * 1024 + (dh + 1) * 512],
                                        pmm, FSC, xsl, AX.mult, AX.add)
                        li.then_inc(S["dv"], 1)
                        cnt["dv"] += 1
                    dv(fvd)
                    down_ev[grp] = cnt["dv"]
                    last_down_ev[0] = cnt["dv"]
                    for ch in range(2):
                        x_upd[hf * 8 + grp * 2 + ch] = cnt["dv"]

                    if call == 2:
                        def fyo(y, hf=hf, grp=grp, n=cnt["dv"]):
                            w(y, "dv", n)
                            for ch in range(2):
                                tg = hf * 8 + grp * 2 + ch
                                y.dma_start(
                                    out=bass.AP(y_out.tensor, tg * 128 * D,
                                                [[D, 128], [1, D]]),
                                    in_=outst[:, (tg % 2) * 1024:(tg % 2 + 1) * 1024],
                                ).then_inc(S["do"], 16)
                                cnt["do"] += 16
                                outdma[tg] = cnt["do"]
                        sy(fyo)
            return x_upd, cnt["dv"]

        outdma = {}

        ffn_es1 = ExitStack()
        if PHASES in ("ln1", "lnh0"):
            bufs1 = {
                "Hh": sb(ffn_es1, "Hh1", [128, 128]),
                "wgT_b": sb(ffn_es1, "wgT1", [128, 128]),
                "wuT_b": sb(ffn_es1, "wuT1", [128, 128]),
                "wdT_f": sb(ffn_es1, "wdT1", [128, 128]),
            }
        else:
            bufs1 = {
                "Hh": sb(ffn_es1, "Hh1", [128, NJ * 1024]),
                "wgT_b": sb(ffn_es1, "wgT1", [128, 3 * 1024]),
                "wuT_b": sb(ffn_es1, "wuT1", [128, 3 * 1024]),
                "wdT_f": sb(ffn_es1, "wdT1", [128, NJ * 1024]),
            }
        x1_upd, ffn1_dv = emit_ffn(1, bufs1, lambda tg: ("dg", x_dg), None)
        if PHASES in ("ffn1", "ln1", "lnh0", "p0", "lnr", "lns", "lna", "lnt",
                      "hh", "wdt"):
            if PHASES == "lnh0":
                def fvend(v, na=cnt["ac"]):
                    w(v, "ac", na)
                    v.tensor_copy(lnstat[:, 96:104], lnstat[:, 24:32]).then_inc(S["dv"], 1)
                    cnt["dv"] += 1
                dv(fvend)

            def fdum(g, nd=cnt["dv"], na=cnt["ac"]):
                w(g, "dv", nd)
                w(g, "ac", na)
                if PHASES in ("lnh0", "ln1"):
                    g.dma_start(
                        out=bass.AP(y_out.tensor, 0, [[8192, 128], [1, 128]]),
                        in_=lnstat[:],
                    ).then_inc(S["dg"], 16)
                    cnt["dg"] += 16
                    g.dma_start(
                        out=bass.AP(y_out.tensor, 128, [[8192, 128], [1, 2048]]),
                        in_=xnbuf[:],
                    ).then_inc(S["dg"], 16)
                    cnt["dg"] += 16
                    g.dma_start(
                        out=bass.AP(y_out.tensor, 128 + 2048, [[8192, 128], [1, 4096]]),
                        in_=xT[:, 0:4096],
                    ).then_inc(S["dg"], 16)
                    cnt["dg"] += 16
                elif PHASES not in ("hh", "wdt"):
                    g.dma_start(
                        out=bass.AP(y_out.tensor, 0, [[D, 128], [128 * D, NCHO], [1, D]]),
                        in_=bass.AP(x_res, 0, [[NCHO * D, 128], [D, NCHO], [1, D]]),
                    ).then_inc(S["dg"], 16)
                    cnt["dg"] += 16
            gp(fdum)

            def ffin1(y, n=cnt["do"], nd=cnt["dg"]):
                y.wait_ge(S["do"], n)
                y.wait_ge(S["dg"], nd)
            sy(ffin1)
            ffn_es1.close()
            return nc
        ffn_es1.close()

        x2_upd = x1_upd

        if not SKIP_ATTN:
            attn_es = ExitStack()
            wqT = sb(attn_es, "wqT", [128, 8 * 128])
            wkT = sb(attn_es, "wkT", [128, 8 * 128])
            wvT = sb(attn_es, "wvT", [128, 8 * 512])
            woT = sb(attn_es, "woT", [128, 8 * 1024])
            x1t = sb(attn_es, "x1t", [128, 3 * 1024])
            qfm = sb(attn_es, "qfm", [128, 2 * 128])
            kfm = sb(attn_es, "kfm", [128, 2 * 128])
            phi = sb(attn_es, "phi", [128, 2 * 2048])
            psi = sb(attn_es, "psi", [128, 2 * 2048])
            phiT = sb(attn_es, "phiT", [128, 2048])
            psiT = sb(attn_es, "psiT", [128, 2048])
            vts = sb(attn_es, "vts", [128, 2 * 520])
            s2m = sb(attn_es, "s2m", [128, 1024])
            Ssb = sb(attn_es, "Ssb", [128, 1536])
            o_asm8 = sb(attn_es, "o_asm8", [128, 8192])
            oT = sb(attn_es, "oT", [128, 1024])

            def emit_ln2():
                for hf in range(2):
                    lb = hf * 64
                    for t in range(8):
                        tg = hf * 8 + t
                        xc = x_res[:, tg * D:(tg + 1) * D]

                        def fg(v, t=t, xc=xc, n=x1_upd[hf * 8 + t]):
                            w(v, "dv", n)
                            v.reduce_sum(lnstat[:, lb + t:lb + t + 1], xc, axis=AXL.X)
                            v.tensor_mul(lnsq[:], xc, xc)
                            v.reduce_sum(lnstat[:, lb + 8 + t:lb + 9 + t], lnsq[:],
                                         axis=AXL.X).then_inc(S["dv"], 1)
                            cnt["dv"] += 1
                        dv(fg)

                    def fv(v):
                        mu, var = lnstat[:, lb + 16:lb + 24], lnstat[:, lb + 24:lb + 32]
                        v.drain()
                        v.tensor_scalar(mu, lnstat[:, lb + 0:lb + 8], 1.0 / D, 0.0, AX.mult, AX.add)
                        v.tensor_scalar(var, lnstat[:, lb + 8:lb + 16], 1.0 / D, 0.0, AX.mult, AX.add)
                        v.tensor_scalar(lnstat[:, lb + 56:lb + 64], lnstat[:, lb + 0:lb + 8],
                                        -1.0 / D, 0.0, AX.mult, AX.add)
                        v.drain()
                        v.tensor_mul(lnstat[:, lb + 32:lb + 40], mu, mu)
                        v.drain()
                        v.tensor_sub(var, var, lnstat[:, lb + 32:lb + 40])
                        v.drain().then_inc(S["dv"], 1)
                        cnt["dv"] += 1
                    dv(fv)

                    def fs(s, n=cnt["dv"]):
                        w(s, "dv", n)
                        s.activation(lnstat[:, lb + 40:lb + 48], lnstat[:, lb + 24:lb + 32], AF.Sqrt,
                                     bias=EPS, scale=1.0)
                        s.drain().then_inc(S["ac"], 1)
                        cnt["ac"] += 1
                    ac(fs)

                    def fv2(v, n=cnt["ac"]):
                        w(v, "ac", n)
                        inv = lnstat[:, lb + 48:lb + 56]
                        v.reciprocal(inv, lnstat[:, lb + 40:lb + 48])
                        v.drain()
                        v.tensor_mul(lnstat[:, lb + 32:lb + 40], lnstat[:, lb + 56:lb + 64],
                                     inv)
                        v.drain().then_inc(S["dv"], 1)
                        cnt["dv"] += 1
                    dv(fv2)
                    inv_dv = cnt["dv"]

                    slot_dma = {}
                    for t in range(8):
                        tg = hf * 8 + t
                        xc = x_res[:, tg * D:(tg + 1) * D]
                        slot = t % 2

                        def fv3(v, xc=xc, t=t, slot=slot, n=inv_dv,
                                pdma=slot_dma.get(t - 2, 0)):
                            w(v, "dv", n)
                            w(v, "do", pdma)
                            v.tensor_scalar(xnbuf[:, slot * 1024:(slot + 1) * 1024], xc,
                                            lnstat[:, lb + 48 + t:lb + 49 + t],
                                            lnstat[:, lb + 32 + t:lb + 33 + t],
                                            AX.mult, AX.add).then_inc(S["dv"], 1)
                            cnt["dv"] += 1
                        dv(fv3)
                        ap_dv = cnt["dv"]

                        items = [(xnbuf[:, slot * 1024 + dc * 128:
                                        slot * 1024 + (dc + 1) * 128],
                                  xnbuf, slot * 1024 + dc * 128) for dc in range(8)]
                        (ek, en), _ = tr_group(
                            items, "A" if slot == 0 else "B",
                            "dv" if slot == 0 else "ac",
                            extra_waits=[("dv", ap_dv)],
                            batch_evac=xnbuf[:, slot * 1024:(slot + 1) * 1024])

                        def fy(y, tg=tg, hf=hf, slot=slot, ek=ek, en=en):
                            w(y, ek, en)
                            y.dma_start(
                                out=bass.AP(cc_x1_in.tensor,
                                            hf * D * 1024 + (tg % 8) * 128,
                                            [[1024, 128], [128 * 1024, 8], [1, 128]]),
                                in_=xnbuf[:, slot * 1024:(slot + 1) * 1024],
                            ).then_inc(S["do"], 16)
                            cnt["do"] += 16
                        sy(fy)
                        slot_dma[t] = cnt["do"]

                    def fcc(g, hf=hf, n=cnt["do"]):
                        w(g, "do", n)
                        g.collective_compute(
                            "AllGather", AX.bypass, replica_groups=GROUPS,
                            ins=[bass.AP(cc_x1_in.tensor, hf * D * 1024,
                                         [[1024, D], [1, 1024]])],
                            outs=[cc_x1_outh[hf]]).then_inc(S["cc"], 1)
                        cnt["cc"] += 1
                    gp(fcc)

            emit_ln2()
            x1_ag = cnt["cc"]

            if PHASES == "x1dump":
                def fx1(g, n=x1_ag):
                    w(g, "cc", n)
                    g.dma_start(
                        out=bass.AP(y_out.tensor, 0, [[2048, 1024], [1, 2048]]),
                        in_=bass.AP(cc_x1_outh[0].tensor, 0, [[1024, 2048], [1, 1024]]),
                    ).then_inc(S["dg"], 16)
                    cnt["dg"] += 16
                gp(fx1)

                def ffx1(y, n=cnt["do"], nd=cnt["dg"]):
                    y.wait_ge(S["do"], n)
                    y.wait_ge(S["dg"], nd)
                sy(ffx1)
                attn_es.close()
                return nc

            def emit_attn_w():
                jobs = []
                jobs.append((bass.AP(wq_n.tensor, 0, [[D, 128], [1, D]]), wqT,
                             [(dc, dc * 128) for dc in range(8)]))
                jobs.append((bass.AP(wk_n.tensor, 0, [[D, 128], [1, D]]), wkT,
                             [(dc, dc * 128) for dc in range(8)]))
                for s_ in range(4):
                    jobs.append((bass.AP(wv_n.tensor, s_ * 128 * D, [[D, 128], [1, D]]),
                                 wvT, [(dc, dc * 512 + s_ * 128) for dc in range(8)]))
                for dcs in range(8):
                    jobs.append((bass.AP(wo_n.tensor, dcs * 128 * D, [[D, 128], [1, D]]),
                                 woT, [(ofc, ofc * 1024 + dcs * 128) for ofc in range(8)]))
                for ji, (src, dstb, tiles) in enumerate(jobs):
                    slot = ji % 6
                    dg_n = stage_load(slot, src)
                    stage = wstage[:, slot * 1024:(slot + 1) * 1024]
                    items = [(stage[:, sc * 128:(sc + 1) * 128], dstb, off)
                             for (sc, off) in tiles]
                    (_ek, _en), pe_n = tr_group(items, "A" if ji % 2 == 0 else "B",
                                                "dv" if ji % 2 == 0 else "ac",
                                                extra_waits=[("dg", dg_n)])
                    wsslot_user[slot] = ("pe", pe_n)

            emit_attn_w()

            def fvts(g, n=ffn1_dv, ng=cnt["gp"]):
                w(g, "dv", n)
                w(g, "gp", ng)
                g.memset(bass.AP(vts, 64,
                                 [[2 * 520, 128], [520, 2], [65, 8]]), 1.0).then_inc(S["gp"], 1)
                cnt["gp"] += 1
            gp(fvts)
            vts_gp = cnt["gp"]

            st = {"x1t": {}, "proj": {}, "projqk": {}, "qkc": {}, "fmsq": {}, "fm_dv": {},
                  "fm_a": {}, "phi": {}, "psi": {}, "vev": {}, "tr": {},
                  "spe": {}, "mask": {}, "ope": {}, "oev": {}, "upd": {},
                  "sc": {}, "odma": {}, "oag": {}}

            def x1t_load(c):
                rank, h, tloc = c // 16, (c % 16) // 8, c % 8

                def fy(y, c=c, rank=rank, h=h, tloc=tloc):
                    w(y, "cc", x1_ag - 1 + h)
                    if c - 3 >= 0:
                        w(y, "pe", st["proj"][c - 3])
                    y.dma_start(
                        out=x1t[:, (c % 3) * 1024:((c % 3) + 1) * 1024],
                        in_=bass.AP(cc_x1_outh[h].tensor,
                                    rank * D * 1024 + tloc * 128,
                                    [[1024, 128], [128 * 1024, 8], [1, 128]]),
                    ).then_inc(S["dy"], 16)
                    cnt["dy"] += 16
                sy(fy)
                st["x1t"][c] = cnt["dy"]

            def proj_qk(c, extra=None):
                def ft(t, c=c, extra=extra):
                    w(t, "dy", st["x1t"][c])
                    if extra is not None:
                        w(t, extra[0], extra[1])
                    if c >= 1:
                        w(t, "dv", st["fm_a"][c - 1])
                    for dc in range(8):
                        lhs = x1t[:, (c % 3) * 1024 + dc * 128:
                                  (c % 3) * 1024 + (dc + 1) * 128]
                        t.matmul(p_qk[:, 0:128], lhs, wqT[:, dc * 128:(dc + 1) * 128],
                                 start=(dc == 0), stop=False)
                        li = t.matmul(p_qk[:, 128:256], lhs,
                                      wkT[:, dc * 128:(dc + 1) * 128],
                                      start=False, stop=(dc == 7))
                    li.then_inc(S["pe"], 1)
                    cnt["pe"] += 1
                pe(ft)
                st["projqk"][c] = cnt["pe"]

            def proj_v(c):
                def ft(t, c=c):
                    w(t, "dy", st["x1t"][c])
                    if c >= 1:
                        w(t, "dv", st["mask"][(c - 1, 1)])
                    for dc in range(8):
                        lhs = x1t[:, (c % 3) * 1024 + dc * 128:
                                  (c % 3) * 1024 + (dc + 1) * 128]
                        li = t.matmul(p_vs[:], lhs, wvT[:, dc * 512:(dc + 1) * 512],
                                      start=(dc == 0), stop=(dc == 7))
                    li.then_inc(S["pe"], 1)
                    cnt["pe"] += 1
                pe(ft)
                st["proj"][c] = cnt["pe"]

            def fm_a(c):
                def fvA(v, c=c):
                    w(v, "pe", st["projqk"][c])
                    v.tensor_copy(qkbuf[:], p_qk[:, 0:256])
                    qk3 = qkbuf[:].rearrange("p (h a) -> p h a", h=16)
                    v.reduce_sum(fmst[:, 0:16], qk3, axis=AXL.X)
                    v.tensor_mul(sq_scr[:, 0:256], qkbuf[:], qkbuf[:])
                    v.reduce_sum(fmst[:, 16:32],
                                 sq_scr[:, 0:256].rearrange("p (h a) -> p h a", h=16),
                                 axis=AXL.X)
                    mu, var = fmst[:, 32:48], fmst[:, 48:64]
                    v.drain()
                    v.tensor_scalar(mu, fmst[:, 0:16], 1.0 / FD, 0.0, AX.mult, AX.add)
                    v.tensor_scalar(var, fmst[:, 16:32], 1.0 / FD, 0.0, AX.mult, AX.add)
                    v.drain()
                    v.tensor_mul(fmst[:, 64:80], mu, mu)
                    v.drain()
                    v.tensor_sub(var, var, fmst[:, 64:80])
                    v.drain().then_inc(S["dv"], 1)
                    cnt["dv"] += 1
                dv(fvA)
                st["fm_a"][c] = cnt["dv"]

            def fm_b(c):
                def fa2(s_, n=st["fm_a"][c]):
                    w(s_, "dv", n)
                    s_.activation(fmst[:, 48:64], fmst[:, 48:64], AF.Sqrt,
                                  bias=EPS, scale=1.0)
                    s_.drain().then_inc(S["ac"], 1)
                    cnt["ac"] += 1
                ac(fa2)

                def fv3(v, c=c, n=cnt["ac"]):
                    w(v, "ac", n)
                    inv = fmst[:, 80:96]
                    v.reciprocal(inv, fmst[:, 48:64])
                    v.drain()
                    slot = (c % 2) * 128
                    for which, (go, bo, dstb) in enumerate(
                            ((0, FD, qfm), (2 * FD, 3 * FD, kfm))):
                        mu_b = bass.AP(fmst, 32 + which * 8, [[96, 128], [1, 8], [0, 16]])
                        inv_b = bass.AP(fmst, 80 + which * 8, [[96, 128], [1, 8], [0, 16]])
                        g_b = bass.AP(gb, go, [[4 * FD, 128], [0, 8], [1, 16]])
                        b_b = bass.AP(gb, bo, [[4 * FD, 128], [0, 8], [1, 16]])
                        dst = sq_scr[:, 512 + which * 128: 512 + (which + 1) * 128]
                        d3 = dst.rearrange("p (h a) -> p h a", h=8)
                        v.tensor_sub(d3, qkbuf[:, which * 128:(which + 1) * 128]
                                     .rearrange("p (h a) -> p h a", h=8), mu_b)
                        v.tensor_mul(d3, d3, inv_b)
                        v.tensor_mul(d3, d3, g_b)
                        li = v.tensor_add(dstb[:, slot:slot + 128]
                                          .rearrange("p (h a) -> p h a", h=8), d3, b_b)
                    li.then_inc(S["dv"], 1)
                    cnt["dv"] += 1
                dv(fv3)
                st["fm_dv"][c] = cnt["dv"]

            def vevac(c):
                def fv(v, c=c):
                    w(v, "pe", st["proj"][c])
                    dst = bass.AP(vts, (c % 2) * 520, [[2 * 520, 128], [65, 8], [1, 64]])
                    v.tensor_copy(dst, p_vs[:].rearrange("p (h d) -> p h d",
                                                         h=8)).then_inc(S["dv"], 1)
                    cnt["dv"] += 1
                dv(fv)
                st["vev"][c] = cnt["dv"]

            def construct(c):
                slot = c % 2

                def fv(v, c=c, slot=slot):
                    w(v, "dv", st["fm_dv"][c])
                    in0 = bass.AP(qfm, slot * 128, [[256, 128], [16, 8], [1, 16], [0, 16]])
                    in1 = bass.AP(qfm, slot * 128, [[256, 128], [16, 8], [0, 16], [1, 16]])
                    v.tensor_tensor(
                        bass.AP(phi, slot * 2048,
                                [[2 * 2048, 128], [256, 8], [16, 16], [1, 16]]),
                        in0, in1, AX.mult).then_inc(S["dv"], 1)
                    cnt["dv"] += 1
                dv(fv)
                st["phi"][c] = cnt["dv"]

                def fg(v, c=c, slot=slot):
                    in0 = bass.AP(kfm, slot * 128, [[256, 128], [16, 8], [1, 16], [0, 16]])
                    in1 = bass.AP(kfm, slot * 128, [[256, 128], [16, 8], [0, 16], [1, 16]])
                    v.tensor_tensor(
                        bass.AP(psi, slot * 2048,
                                [[2 * 2048, 128], [256, 8], [16, 16], [1, 16]]),
                        in0, in1, AX.mult).then_inc(S["dv"], 1)
                    cnt["dv"] += 1
                dv(fg)
                st["psi"][c] = cnt["dv"]

            def transposes(c):
                slot = c % 2
                srcs = [bass.AP(phi, slot * 2048 + k * 128, [[2 * 2048, 128], [1, 128]])
                        for k in range(16)]
                srcs += [bass.AP(psi, slot * 2048 + k * 128, [[2 * 2048, 128], [1, 128]])
                         for k in range(16)]
                dsts = [(phiT, k * 128) for k in range(16)] + \
                       [(psiT, k * 128) for k in range(16)]
                for gi in range(4):
                    items = [(srcs[gi * 8 + k], dsts[gi * 8 + k][0], dsts[gi * 8 + k][1])
                             for k in range(8)]
                    ew = []
                    if gi == 0:
                        ew = [("dv", st["psi"][c])]
                    dstb = (phiT, psiT)[gi // 2]
                    boff = (gi % 2) * 1024
                    tr_group(items, "A" if gi % 2 == 0 else "B",
                             "dv" if gi % 2 == 0 else "ac", extra_waits=ew,
                             batch_evac=dstb[:, boff:boff + 1024])
                st["tr"][c] = cnt["pe"]
                st["trlast"] = (kk.trfree["A"], kk.trfree["B"])

            def srounds(c, r):
                def ft(t, c=c, r=r):
                    if r == 0:
                        w(t, kk.trfree["A"][0], kk.trfree["A"][1])
                        w(t, kk.trfree["B"][0], kk.trfree["B"][1])
                        w(t, "dv", st["vev"][c])
                    else:
                        w(t, "dv", st["mask"][(c, 0)])
                    for hh in range(4):
                        h = r * 4 + hh
                        for half in range(2):
                            ti = h * 2 + half
                            li = t.matmul(p_vs[:, hh * 128:(hh + 1) * 128],
                                          psiT[:, ti * 128:(ti + 1) * 128],
                                          phiT[:, ti * 128:(ti + 1) * 128],
                                          start=(hh == 0 and half == 0),
                                          stop=(hh == 3 and half == 1))
                    li.then_inc(S["pe"], 1)
                    cnt["pe"] += 1
                pe(ft)
                st["spe"][(c, r)] = cnt["pe"]

            def mask(c, r):
                def fv(v, c=c, r=r):
                    w(v, "pe", st["spe"][(c, r)])
                    w(v, "gp", const_gp)
                    tri_b = bass.AP(trimask, 0, [[128, 128], [0, 4], [1, 128]])
                    v.tensor_tensor(
                        s2m[:, r * 512:(r + 1) * 512].rearrange("p (h i) -> p h i", h=4),
                        p_vs[:].rearrange("p (h i) -> p h i", h=4),
                        tri_b, AX.mult).then_inc(S["dv"], 1)
                    cnt["dv"] += 1
                dv(fv)
                st["mask"][(c, r)] = cnt["dv"]

            def oround(c, r):
                def ft(t, c=c, r=r):
                    if c == 0:
                        w(t, "gp", vts_gp)
                    w(t, "dv", st["mask"][(c, r)])
                    if c >= 1:
                        w(t, "dv", st["sc"][c - 1])
                    if r == 1:
                        w(t, "dv", st["oev"][(c, 0)])
                    elif c >= 1:
                        w(t, "dv", st["oev"][(c - 1, 1)])
                    for hh in range(4):
                        h = r * 4 + hh
                        vsl = bass.AP(vts, (c % 2) * 520 + h * 65, [[2 * 520, 128], [1, 65]])
                        li = t.matmul(p_o[:, hh * 65:(hh + 1) * 65],
                                      s2m[:, (r * 4 + hh) * 128:(r * 4 + hh + 1) * 128],
                                      vsl, start=(hh == 0), stop=(c == 0 and hh == 3))
                        if c >= 1:
                            for half in range(2):
                                ti = h * 2 + half
                                sb_, sl = ti // 7, ti % 7
                                li = t.matmul(p_o[:, hh * 65:(hh + 1) * 65],
                                              phiT[:, ti * 128:(ti + 1) * 128],
                                              bass.AP(Ssb, sb_ * 512 + sl * 65,
                                                      [[1536, 128], [1, 65]]),
                                              start=False, stop=(hh == 3 and half == 1))
                    li.then_inc(S["pe"], 1)
                    cnt["pe"] += 1
                pe(ft)
                st["ope"][(c, r)] = cnt["pe"]

            def upd(c, r):
                def ft(t, c=c, r=r):
                    w(t, "dv", st["psi"][c])
                    for hh in range(4):
                        h = r * 4 + hh
                        vsl = bass.AP(vts, (c % 2) * 520 + h * 65, [[2 * 520, 128], [1, 65]])
                        for half in range(2):
                            ti = h * 2 + half
                            sb_, sl = ti // 7, ti % 7
                            li = t.matmul(bass.AP(p_S[sb_], sl * 65, [[512, 128], [1, 65]]),
                                          bass.AP(psi, (c % 2) * 2048 + ti * 128,
                                                  [[2 * 2048, 128], [1, 128]]),
                                          vsl,
                                          start=(c == 0 and sl == 0),
                                          stop=(c == NCH - 1 and ti in (6, 13, 15)),
                                          skip_group_check=True)
                    li.then_inc(S["pe"], 1)
                    cnt["pe"] += 1
                pe(ft)
                st["upd"][(c, r)] = cnt["pe"]

            def o_evac(c, r):
                def fv(v, c=c, r=r):
                    w(v, "pe", st["ope"][(c, r)])
                    if c >= 2 and r == 0:
                        w(v, "do", st["odma"][c - 2])
                    zv = bass.AP(p_o, 64, [[512, 128], [65, 4]])
                    v.tensor_scalar(zbuf[:, 0:4], zv, 1.0, EPS, AX.mult, AX.add)
                    v.drain()
                    v.reciprocal(zbuf[:, 4:8], zbuf[:, 0:4])
                    v.drain()
                    ov = bass.AP(p_o, 0, [[512, 128], [65, 4], [1, 64]])
                    zr_b = bass.AP(zbuf, 4, [[16, 128], [1, 4], [0, 64]])
                    v.tensor_tensor(
                        bass.AP(oev, (c % 2) * 512 + r * 256,
                                [[2 * 512, 128], [64, 4], [1, 64]]),
                        ov, zr_b, AX.mult).then_inc(S["dv"], 1)
                    cnt["dv"] += 1
                dv(fv)
                st["oev"][(c, r)] = cnt["dv"]

            def snapshot(c):
                def fv(v, c=c):
                    w(v, "pe", st["upd"][(c, 1)])
                    for i, ncol in ((0, 455), (1, 455), (2, 130)):
                        li = v.tensor_copy(Ssb[:, i * 512:i * 512 + ncol],
                                           p_S[i][:, 0:ncol])
                    li.then_inc(S["dv"], 1)
                    cnt["dv"] += 1
                dv(fv)
                st["sc"][c] = cnt["dv"]

            def o_dma(c):
                def fy(y, c=c):
                    w(y, "dv", st["oev"][(c, 1)])
                    y.dma_start(
                        out=bass.AP(cc_o_in.tensor, c * 128 * 512, [[512, 128], [1, 512]]),
                        in_=oev[:, (c % 2) * 512:((c % 2) + 1) * 512],
                    ).then_inc(S["do"], 16)
                    cnt["do"] += 16
                sy(fy)
                st["odma"][c] = cnt["do"]

            if PHASES == "x1t3":
                for c in range(3):
                    x1t_load(c)

                def fxp(g, ndy=cnt["dy"]):
                    w(g, "dy", ndy)
                    g.dma_start(
                        out=bass.AP(y_out.tensor, 0, [[16384, 128], [1, 3072]]),
                        in_=x1t[:, 0:3072]).then_inc(S["dg"], 16)
                    cnt["dg"] += 16
                gp(fxp)

                def ffxp(y, n=cnt["do"], nd=cnt["dg"]):
                    y.wait_ge(S["do"], n)
                    y.wait_ge(S["dg"], nd)
                sy(ffxp)
                attn_es.close()
                return nc

            for c in range(3):
                x1t_load(c)
            proj_qk(0, extra=("dv", ffn1_dv))
            proj_v(0)
            fm_a(0)
            fm_b(0)
            vevac(0)
            construct(0)
            for c in range(1, NCH + 1):
                cp = c - 1
                if c < NCH:
                    proj_qk(c)
                transposes(cp)
                srounds(cp, 0)
                mask(cp, 0)
                if c < NCH:
                    fm_a(c)
                srounds(cp, 1)
                mask(cp, 1)
                oround(cp, 0)
                upd(cp, 0)
                o_evac(cp, 0)
                if c < NCH:
                    fm_b(c)
                    construct(c)
                if c < NCH:
                    proj_v(c)
                    if c + 2 < NCH and (c + 2) not in st["x1t"]:
                        x1t_load(c + 2)
                oround(cp, 1)
                upd(cp, 1)
                o_evac(cp, 1)
                snapshot(cp)
                o_dma(cp)
                if PHASES == "a0" and c == 1:
                    def fa0(g, nd=cnt["dv"], na=cnt["ac"], np_=cnt["pe"],
                            ng=cnt["gp"], ndo=cnt["do"], ndy=cnt["dy"]):
                        w(g, "dv", nd)
                        w(g, "ac", na)
                        w(g, "pe", np_)
                        w(g, "gp", ng)
                        w(g, "do", ndo)
                        w(g, "dy", ndy)
                        dumps = [(x1t, 0, 1024), (qfm, 0, 128), (kfm, 0, 128),
                                 (phi, 0, 2048), (psi, 0, 2048), (s2m, 0, 1024),
                                 (vts, 0, 520), (oev, 0, 512), (zbuf, 0, 16),
                                 (phiT, 0, 2048), (psiT, 0, 2048)]
                        off = 0
                        for buf, so, ncol in dumps:
                            g.dma_start(
                                out=bass.AP(y_out.tensor, off,
                                            [[16384, 128], [1, ncol]]),
                                in_=buf[:, so:so + ncol]).then_inc(S["dg"], 16)
                            cnt["dg"] += 16
                            off += ncol
                    gp(fa0)

                    def ffa0(y, n=cnt["do"], nd=cnt["dg"]):
                        y.wait_ge(S["do"], n)
                        y.wait_ge(S["dg"], nd)
                    sy(ffa0)
                    attn_es.close()
                    return nc
                if c < NCH:
                    vevac(c)
                if cp % 8 == 7:
                    q = cp // 8

                    def fcc2(g, q=q, n=st["odma"][cp]):
                        w(g, "do", n)
                        g.collective_compute(
                            "AllGather", AX.bypass, replica_groups=GROUPS,
                            ins=[bass.AP(cc_o_in.tensor, q * 1024 * 512,
                                         [[512, 1024], [1, 512]])],
                            outs=[cc_o_out[q]]).then_inc(S["cc"], 1)
                        cnt["cc"] += 1
                    gp(fcc2)
                    st["oag"][q] = cnt["cc"]

            for c in range(NCH):
                assert c in st["x1t"], f"x1t chunk {c} never loaded"

            if PHASES == "odump":
                def fod(g, n=st["odma"][NCH - 1]):
                    w(g, "do", n)
                    g.dma_start(
                        out=bass.AP(y_out.tensor, 0, [[512, 4096], [1, 512]]),
                        in_=bass.AP(cc_o_in.tensor, 0, [[512, 4096], [1, 512]]),
                    ).then_inc(S["dg"], 16)
                    cnt["dg"] += 16
                gp(fod)

                def ffo(y, n=cnt["do"], nd=cnt["dg"]):
                    y.wait_ge(S["do"], n)
                    y.wait_ge(S["dg"], nd)
                sy(ffo)
                attn_es.close()
                return nc

            scan_pe = cnt["pe"]
            wo_pe = {}
            parv = nc.axis_index(bass.AxisInfo(2, 1))
            pconds = [parv == 0, parv == 1]

            def fpar(g):
                g.compute_val(pconds[0])
                g.compute_val(pconds[1])
            gp(fpar)
            for qi in range(2):
                def fga(g, qi=qi):
                    for pv in range(2):
                        q = pv * 2 + qi
                        w(g, "cc", st["oag"][q])
                        if qi == 1:
                            w(g, "pe", wo_pe[7])
                        for rank in range(2):
                            g.dma_start(
                                out=bass.AP(o_asm8, rank * 512,
                                            [[8192, 128], [1024, 8], [1, 512]]),
                                in_=bass.AP(cc_o_out[q].tensor, rank * 1024 * 512,
                                            [[512, 128], [128 * 512, 8], [1, 512]]),
                                cond=pconds[pv],
                            ).then_inc(S["dg"], 16)
                            cnt["dg"] += 16
                gp(fga)
                asm_dg = cnt["dg"]

                for tl in range(8):
                    t_ = qi * 8 + tl
                    items = [(o_asm8[:, tl * 1024 + ofc * 128:
                                     tl * 1024 + (ofc + 1) * 128], oT, ofc * 128)
                             for ofc in range(8)]
                    ew = [("dg", asm_dg)]
                    if t_ == 0:
                        ew.append(("pe", scan_pe))
                        ew.append(("dv", st["sc"][NCH - 1]))
                    (ek, en), _pe_n = tr_group(items, "A" if t_ % 2 == 0 else "B",
                                               "dv" if t_ % 2 == 0 else "ac",
                                               extra_waits=ew, batch_evac=oT[:])

                    def ft2(t, t_=t_, ek=ek, en=en,
                            pf=0 if t_ == 0 else None):
                        w(t, ek, en)
                        if t_ >= 1:
                            w(t, "dv", x2_upd.get(t_ - 1, 0))
                        for ofc in range(8):
                            lhs = oT[:, ofc * 128:(ofc + 1) * 128]
                            for dh in range(2):
                                li = t.matmul(p_mm[2 + dh][:], lhs,
                                              woT[:, ofc * 1024 + dh * 512:
                                                  ofc * 1024 + (dh + 1) * 512],
                                              start=(ofc == 0), stop=(ofc == 7))
                        li.then_inc(S["pe"], 1)
                        cnt["pe"] += 1
                    pe(ft2)
                    wo_pe[t_] = cnt["pe"]

                    def fvr(v, t_=t_, n=cnt["pe"]):
                        w(v, "pe", n)
                        for dh in range(2):
                            xsl = x_res[:, t_ * D + dh * 512: t_ * D + (dh + 1) * 512]
                            li = v.tensor_add(xsl, xsl, p_mm[2 + dh][:])
                        li.then_inc(S["dv"], 1)
                        cnt["dv"] += 1
                    dv(fvr)
                    x2_upd[t_] = cnt["dv"]

            attn_es.close()
            ffn2_mm_free = ("dv", cnt["dv"])
        else:
            ffn2_mm_free = ("dv", ffn1_dv)

        if PHASES == "attno":
            def fdat(g, nd=cnt["dv"], na=cnt["ac"]):
                w(g, "dv", nd)
                w(g, "ac", na)
                g.dma_start(
                    out=bass.AP(y_out.tensor, 0, [[D, 128], [128 * D, NCHO], [1, D]]),
                    in_=bass.AP(x_res, 0, [[NCHO * D, 128], [D, NCHO], [1, D]]),
                ).then_inc(S["dg"], 16)
                cnt["dg"] += 16
            gp(fdat)

            def ffat(y, n=cnt["do"], nd=cnt["dg"]):
                y.wait_ge(S["do"], n)
                y.wait_ge(S["dg"], nd)
            sy(ffat)
            return nc

        ffn_es2 = ExitStack()
        bufs2 = {
            "Hh": sb(ffn_es2, "Hh2", [128, NJ * 1024]),
            "wgT_b": sb(ffn_es2, "wgT2", [128, 3 * 1024]),
            "wuT_b": sb(ffn_es2, "wuT2", [128, 3 * 1024]),
            "wdT_f": sb(ffn_es2, "wdT2", [128, NJ * 1024]),
        }
        emit_ffn(2, bufs2, lambda tg: ("dv", x2_upd[tg]), ffn2_mm_free)
        ffn_es2.close()

        def ffin(y, n=cnt["do"]):
            y.wait_ge(S["do"], n)
        sy(ffin)

    return nc


def kernel(L0_attn_x, L0_attn_Wq, L0_attn_Wk, L0_attn_Wv, L0_attn_Wo,
           L0_attn_gamma, L0_attn_beta, L0_ffn_w_gate, L0_ffn_w_up, L0_ffn_w_down):
    x = np.ascontiguousarray(np.asarray(L0_attn_x, dtype=np.float32))
    wq = np.ascontiguousarray(np.asarray(L0_attn_Wq, dtype=np.float32))
    wk = np.ascontiguousarray(np.asarray(L0_attn_Wk, dtype=np.float32))
    wv = np.ascontiguousarray(np.asarray(L0_attn_Wv, dtype=np.float32))
    wo = np.ascontiguousarray(np.asarray(L0_attn_Wo, dtype=np.float32))
    gam = np.ascontiguousarray(np.asarray(L0_attn_gamma, dtype=np.float32))
    bet = np.ascontiguousarray(np.asarray(L0_attn_beta, dtype=np.float32))
    wg = np.ascontiguousarray(np.asarray(L0_ffn_w_gate, dtype=np.int32))
    wu = np.ascontiguousarray(np.asarray(L0_ffn_w_up, dtype=np.int32))
    wd = np.ascontiguousarray(np.asarray(L0_ffn_w_down, dtype=np.int32))

    nc = build_nc()
    in_maps = []
    for c in range(8):
        b, p = c // 2, c % 2
        in_maps.append({
            "x_own": np.ascontiguousarray(x[b, p * TOK:(p + 1) * TOK, :]),
            "wq_own": np.ascontiguousarray(wq[p * 128:(p + 1) * 128, :]),
            "wk_own": np.ascontiguousarray(wk[p * 128:(p + 1) * 128, :]),
            "wv_own": np.ascontiguousarray(wv[p * 512:(p + 1) * 512, :]),
            "wo": wo, "gamma": gam, "beta": bet,
            "w_gate": wg, "w_up": wu, "w_down": wd,
        })
    res = run_bass_kernel_spmd(nc, in_maps, core_ids=list(range(8)))
    out = np.empty((B, L, D), dtype=np.float32)
    for c in range(8):
        b, p = c // 2, c % 2
        out[b, p * TOK:(p + 1) * TOK, :] = res.results[c]["y_own"]
    return out


# version 14
# speedup vs baseline: 1.2404x; 1.2334x over previous
import numpy as np
from contextlib import ExitStack

import concourse.bass as bass
import concourse.mybir as mybir
from concourse.bass_utils import run_bass_kernel_spmd

F32 = mybir.dt.float32
BF16 = mybir.dt.bfloat16
I32 = mybir.dt.int32
AX = mybir.AluOpType
AF = mybir.ActivationFunctionType
AXL = mybir.AxisListType

B, L, D = 4, 4096, 1024
HID = 3072
FD, HD = 16, 64
TOK = 2048
NCHO = 16
NCH = 32
NJ = HID // 128
EPS = 1e-5
ZK = 1.0 / 256.0
QSC = FD ** -0.5
FSC = 0.5 / (127.0 * 127.0)

SKIP_ATTN = False
PHASES = "all"
LNCH = 8
GROUPS = [[0, 1], [2, 3], [4, 5], [6, 7]]


class K:

    def __init__(self, nc, block, S):
        self.nc, self.block, self.S = nc, block, S
        self.cnt = {k: 0 for k in S}
        self.trfree = {}

    def inc(self, inst, key, k=1):
        inst.then_inc(self.S[key], k)
        self.cnt[key] += k
        return self.cnt[key]

    def bump(self, key, k=1):
        self.cnt[key] += k
        return self.cnt[key]

    @staticmethod
    def w(e, S, key, n):
        if n > 0:
            e.wait_ge(S[key], n)


def build_nc():
    nc = bass.Bass("TRN2", target_bir_lowering=False, detect_race_conditions=False)

    x_in = nc.dram_tensor("x_own", [TOK, D], F32, kind="ExternalInput").ap()
    wq_n = nc.dram_tensor("wq_own", [128, D], F32, kind="ExternalInput").ap()
    wk_n = nc.dram_tensor("wk_own", [128, D], F32, kind="ExternalInput").ap()
    wv_n = nc.dram_tensor("wv_own", [512, D], F32, kind="ExternalInput").ap()
    wo_n = nc.dram_tensor("wo", [D, D], F32, kind="ExternalInput").ap()
    gam_in = nc.dram_tensor("gamma", [FD], F32, kind="ExternalInput").ap()
    bet_in = nc.dram_tensor("beta", [FD], F32, kind="ExternalInput").ap()
    wg_n = nc.dram_tensor("w_gate", [HID, D], I32, kind="ExternalInput").ap()
    wu_n = nc.dram_tensor("w_up", [HID, D], I32, kind="ExternalInput").ap()
    wd_n = nc.dram_tensor("w_down", [D, HID], I32, kind="ExternalInput").ap()
    y_out = nc.dram_tensor("y_own", [TOK, D], F32, kind="ExternalOutput").ap()

    wgT_sc = nc.dram_tensor("wgT_sc", [NJ, 8, 128, 128], BF16).ap()
    wuT_sc = nc.dram_tensor("wuT_sc", [NJ, 8, 128, 128], BF16).ap()
    wdT_sc = nc.dram_tensor("wdT_sc", [NJ, 8, 128, 128], BF16).ap()
    cc_x1_in = nc.dram_tensor("cc_x1_in", [2, D, 1024], BF16).ap()
    cc_x1_outh = [nc.dram_tensor(f"cc_x1_out{h}", [2 * D, 1024], BF16).ap()
                  for h in range(2)]
    cc_o_in = nc.dram_tensor("cc_o_in", [L, 512], BF16).ap()
    cc_o_out = [nc.dram_tensor(f"cc_o_out{q}", [2048, 512], BF16).ap() for q in range(4)]

    es = ExitStack()

    def sb(e, name, shape, dt=BF16):
        return e.enter_context(nc.sbuf_tensor(name, shape, dt))

    with es:
        x_res = sb(es, "x_res", [128, NCHO * D])
        xT = sb(es, "xT", [128, 8 * TOK])
        xnbuf = sb(es, "xnbuf", [128, 2 * 1024])
        wstage = sb(es, "wstage", [128, 6 * 1024])
        sq_scr = sb(es, "sq_scr", [128, 1024], F32)
        lnsq = sb(es, "lnsq", [128, 1024])
        lnstat = sb(es, "lnstat", [128, 128], F32)
        outst = sb(es, "outst", [128, 2 * 1024], F32)
        oev = sb(es, "oev", [128, 2 * 512])
        ident = sb(es, "ident", [128, 128])
        trimask = sb(es, "trimask", [128, 128])
        gb = sb(es, "gb", [128, 4 * FD], F32)
        gbr = sb(es, "gbr", [128, 2 * FD], F32)
        zbuf = sb(es, "zbuf", [128, 16], F32)
        fmst = sb(es, "fmst", [128, 96], F32)
        qkbuf = sb(es, "qkbuf", [128, 256], F32)
        cst = sb(es, "cst", [128, 4], F32)

        p_trA = es.enter_context(nc.psum_tensor("p_trA", [128, 1024], BF16))
        p_trB = es.enter_context(nc.psum_tensor("p_trB", [128, 1024], BF16))
        p_mm = [es.enter_context(nc.psum_tensor(f"p_mm{i}", [128, 512], F32))
                for i in range(6)]
        p_qk, p_vs = p_mm[0], p_mm[1]
        p_S = [p_mm[2], p_mm[3], p_mm[4]]
        p_o = p_mm[5]

        semnames = ["pe", "dv", "ac", "gp", "dy", "dg", "do", "cc"]
        S = {k: es.enter_context(nc.semaphore("s_" + k)) for k in semnames}
        block = es.enter_context(nc.Block())
        nc.cache_partition_id()
        for ci, cv in enumerate((EPS, 1.0 / 127.0, 0.0, 1.0)):
            nc.const_aps.register(F32, cv, cst[:, ci:ci + 1])
        kk = K(nc, block, S)
        cnt = kk.cnt

        constgp = [0]

        def w(e, key, n):
            if n > 0:
                e.wait_ge(S[key], n)

        def pe(f): block.tensor(f)
        def dv(f): block.vector(f)
        def ac(f): block.scalar(f)
        def gp(f): block.gpsimd(f)
        def sy(f): block.sync(f)

        def tr_group(items, bank, evac_eng, extra_waits=(), batch_evac=None):
            pbank = p_trA if bank == "A" else p_trB
            free = kk.trfree.get(bank)
            waits = list(extra_waits)
            if free is not None:
                waits.append(free)

            def ft(t, items=items, pbank=pbank, waits=tuple(waits)):
                w(t, "gp", constgp[0])
                for key, n in waits:
                    w(t, key, n)
                li = None
                for i, (src, _d, _o) in enumerate(items):
                    li = t.transpose(pbank[:, i * 128:(i + 1) * 128], src, ident[:])
                li.then_inc(S["pe"], 1)
                cnt["pe"] += 1
            pe(ft)
            pe_n = cnt["pe"]

            if PHASES == "lnt":
                res = ("dv", 0)
                kk.trfree[bank] = ("pe", pe_n)
                return res, pe_n
            if evac_eng == "dv":
                def fe(v, items=items, pbank=pbank, n=pe_n, batch=batch_evac):
                    w(v, "pe", n)
                    if batch is not None:
                        li = v.tensor_copy(batch, pbank[:, 0:128 * len(items)])
                    else:
                        for i, (_s, dstb, off) in enumerate(items):
                            li = v.tensor_copy(dstb[:, off:off + 128],
                                               pbank[:, i * 128:(i + 1) * 128])
                    li.then_inc(S["dv"], 1)
                    cnt["dv"] += 1
                dv(fe)
                res = ("dv", cnt["dv"])
            else:
                def fe(s, items=items, pbank=pbank, n=pe_n, batch=batch_evac):
                    w(s, "pe", n)
                    if batch is not None:
                        li = s.copy(batch, pbank[:, 0:128 * len(items)])
                    else:
                        for i, (_s2, dstb, off) in enumerate(items):
                            li = s.copy(dstb[:, off:off + 128],
                                        pbank[:, i * 128:(i + 1) * 128])
                    li.then_inc(S["ac"], 1)
                    cnt["ac"] += 1
                ac(fe)
                res = ("ac", cnt["ac"])
            kk.trfree[bank] = res
            return res, pe_n

        def f00(v):
            for ci, cv in enumerate((EPS, 1.0 / 127.0, 0.0, 1.0)):
                v.memset(cst[:, ci:ci + 1], cv)
            if PHASES == "lnh0":
                v.memset(lnstat[:], 777.0)
            v.memset(ident[:], 1.0)
            v.memset(trimask[:], 1.0).then_inc(S["dv"], 1)
            cnt["dv"] += 1
        dv(f00)
        pre_dv = cnt["dv"]

        def f0(g):
            g.dma_start(
                out=bass.AP(x_res, 0, [[NCHO * D, 128], [D, NCHO], [1, D]]),
                in_=bass.AP(x_in.tensor, 0, [[D, 128], [128 * D, NCHO], [1, D]]),
                ).then_inc(S["dg"], 16)
            cnt["dg"] += 16
            g.dma_start(out=bass.AP(gbr, 0, [[2 * FD, 128], [1, FD]]),
                        in_=bass.AP(gam_in.tensor, 0, [[0, 128], [1, FD]])).then_inc(S["dg"], 16)
            cnt["dg"] += 16
            g.dma_start(out=bass.AP(gbr, FD, [[2 * FD, 128], [1, FD]]),
                        in_=bass.AP(bet_in.tensor, 0, [[0, 128], [1, FD]])).then_inc(S["dg"], 16)
            cnt["dg"] += 16
            w(g, "dv", pre_dv)
            g.affine_select(out=ident[:], in_=ident[:], pattern=[[-1, 128]],
                            compare_op=AX.is_equal, fill=0.0, base=0,
                            channel_multiplier=1).then_inc(S["gp"], 1)
            cnt["gp"] += 1
            g.wait_ge(S["gp"], cnt["gp"])
            g.affine_select(out=trimask[:], in_=trimask[:], pattern=[[1, 128]],
                            compare_op=AX.is_ge, fill=0.0, base=0,
                            channel_multiplier=-1).then_inc(S["gp"], 1)
            cnt["gp"] += 1
        gp(f0)
        x_dg = 16
        gb_dg = 48
        const_gp = cnt["gp"]
        constgp[0] = const_gp

        def f0v(v):
            w(v, "dg", gb_dg)
            v.tensor_scalar(gb[:, 0:FD], gbr[:, 0:FD], QSC, 0.0, AX.mult, AX.add)
            v.tensor_scalar(gb[:, FD:2 * FD], gbr[:, FD:2 * FD], QSC, 0.0, AX.mult, AX.add)
            v.tensor_copy(gb[:, 2 * FD:3 * FD], gbr[:, 0:FD])
            v.tensor_copy(gb[:, 3 * FD:4 * FD], gbr[:, FD:2 * FD]).then_inc(S["dv"], 1)
            cnt["dv"] += 1
        dv(f0v)
        gb_dv = cnt["dv"]

        def emit_ln_half(hf, dep):
            lb = hf * 64
            nch = LNCH if PHASES in ("lnh0", "lnr", "lns", "lna", "lnt") else 8
            for t in range(nch):
                tg = hf * 8 + t
                xc = x_res[:, tg * D:(tg + 1) * D]
                dk, dn = dep(tg)

                def fg(v, t=t, xc=xc, dk=dk, dn=dn):
                    w(v, dk, dn)
                    v.reduce_sum(lnstat[:, lb + t:lb + t + 1], xc, axis=AXL.X)
                    v.tensor_mul(lnsq[:], xc, xc)
                    v.reduce_sum(lnstat[:, lb + 8 + t:lb + 9 + t], lnsq[:],
                                 axis=AXL.X).then_inc(S["dv"], 1)
                    cnt["dv"] += 1
                dv(fg)
            if PHASES == "lnr":
                return ("dv", cnt["dv"]), ("ac", cnt["ac"])

            def fv(v):
                mu, var = lnstat[:, lb + 16:lb + 24], lnstat[:, lb + 24:lb + 32]
                v.drain()
                v.tensor_scalar(mu, lnstat[:, lb + 0:lb + 8], 1.0 / D, 0.0, AX.mult, AX.add)
                v.tensor_scalar(var, lnstat[:, lb + 8:lb + 16], 1.0 / D, 0.0, AX.mult, AX.add)
                v.tensor_scalar(lnstat[:, lb + 56:lb + 64], lnstat[:, lb + 0:lb + 8],
                                -1.0 / D, 0.0, AX.mult, AX.add)
                v.drain()
                v.tensor_mul(lnstat[:, lb + 32:lb + 40], mu, mu)
                v.drain()
                v.tensor_sub(var, var, lnstat[:, lb + 32:lb + 40])
                if PHASES == "lnh0":
                    v.drain()
                    v.tensor_copy(lnstat[:, 80:88], var)
                v.drain().then_inc(S["dv"], 1)
                cnt["dv"] += 1
            dv(fv)

            def fs(s, n=cnt["dv"]):
                w(s, "dv", n)
                if PHASES == "lnh0":
                    s.activation(lnstat[:, 72:80], lnstat[:, lb + 24:lb + 32],
                                 AF.Copy, bias=0.0, scale=1.0)
                s.activation(lnstat[:, lb + 40:lb + 48], lnstat[:, lb + 24:lb + 32], AF.Sqrt,
                             bias=EPS, scale=1.0)
                s.drain().then_inc(S["ac"], 1)
                cnt["ac"] += 1
            ac(fs)

            def fv2(v, n=cnt["ac"]):
                w(v, "ac", n)
                inv = lnstat[:, lb + 48:lb + 56]
                v.reciprocal(inv, lnstat[:, lb + 40:lb + 48])
                v.drain()
                v.tensor_mul(lnstat[:, lb + 32:lb + 40], lnstat[:, lb + 56:lb + 64],
                             inv)
                v.drain().then_inc(S["dv"], 1)
                cnt["dv"] += 1
            dv(fv2)
            inv_dv = cnt["dv"]
            if PHASES == "lns":
                return ("dv", cnt["dv"]), ("ac", cnt["ac"])

            for t in range(nch):
                tg = hf * 8 + t
                xc = x_res[:, tg * D:(tg + 1) * D]
                slot = t % 2

                def fv3(v, xc=xc, t=t, slot=slot, n=inv_dv):
                    w(v, "dv", n)
                    v.tensor_scalar(xnbuf[:, slot * 1024:(slot + 1) * 1024], xc,
                                    lnstat[:, lb + 48 + t:lb + 49 + t], lnstat[:, lb + 32 + t:lb + 33 + t],
                                    AX.mult, AX.add).then_inc(S["dv"], 1)
                    cnt["dv"] += 1
                dv(fv3)
                ap_dv = cnt["dv"]
                if PHASES == "lna":
                    continue

                items = [(xnbuf[:, slot * 1024 + dc * 128: slot * 1024 + (dc + 1) * 128],
                          xT, dc * TOK + hf * 1024 + t * 128) for dc in range(8)]
                tr_group(items, "A" if slot == 0 else "B",
                         "dv" if slot == 0 else "ac",
                         extra_waits=[("dv", ap_dv)])
            return (kk.trfree.get("A", ("dv", 0)), kk.trfree.get("B", ("ac", 0)))

        wsslot_user = {}

        def stage_load(slot, src, cast_queue="dg"):
            prev = wsslot_user.get(slot)

            def fy(g, slot=slot, src=src, prev=prev):
                if prev is not None:
                    w(g, prev[0], prev[1])
                g.dma_start(out=wstage[:, slot * 1024:(slot + 1) * 1024],
                            in_=src).then_inc(S["dg"], 16)
                cnt["dg"] += 16
            gp(fy)
            return cnt["dg"]

        spill_do = {}

        def wslab_transpose(kind, j, dstb, dst_off, spill_sc, slot, busy_wait):
            if kind == "g":
                src = bass.AP(wg_n.tensor, j * 128 * D, [[D, 128], [1, D]])
            elif kind == "u":
                src = bass.AP(wu_n.tensor, j * 128 * D, [[D, 128], [1, D]])
            else:
                src = bass.AP(wd_n.tensor, j * 128, [[HID, 128], [128 * HID, 8], [1, 128]])
            dg_n = stage_load(slot, src)
            stage = wstage[:, slot * 1024:(slot + 1) * 1024]
            items = [(stage[:, dc * 128:(dc + 1) * 128], dstb, dst_off + dc * 128)
                     for dc in range(8)]
            ew = [("dg", dg_n)]
            psp = spill_do.get((kind, dst_off))
            if psp is not None:
                ew.append(("do", psp))
            if busy_wait is not None:
                ew.append(busy_wait)
            (ek, en), pe_n = tr_group(items, "A", "dv", extra_waits=ew,
                                      batch_evac=dstb[:, dst_off:dst_off + 1024])
            wsslot_user[slot] = ("pe", pe_n)

            def fsp(y, dstb=dstb, dst_off=dst_off, spill_sc=spill_sc, j=j, en=en):
                w(y, "dv", en)
                y.dma_start(out=bass.AP(spill_sc.tensor, j * 8 * 128 * 128,
                                        [[128, 128], [128 * 128, 8], [1, 128]]),
                            in_=dstb[:, dst_off:dst_off + 1024]).then_inc(S["do"], 16)
                cnt["do"] += 16
            sy(fsp)
            spill_do[(kind, dst_off)] = cnt["do"]
            return ("dv", en)

        def wslab_stream(sc, j, dstb, dst_off, busy_wait, spill_do):
            def fy(y, sc=sc, j=j, dstb=dstb, dst_off=dst_off,
                   busy_wait=busy_wait, spill_do=spill_do):
                w(y, "do", spill_do)
                if busy_wait is not None:
                    w(y, busy_wait[0], busy_wait[1])
                y.dma_start(out=dstb[:, dst_off:dst_off + 1024],
                            in_=bass.AP(sc.tensor, j * 8 * 128 * 128,
                                        [[128, 128], [128 * 128, 8], [1, 128]]),
                            ).then_inc(S["dy"], 16)
                cnt["dy"] += 16
            sy(fy)
            return ("dy", cnt["dy"])

        spill_done = {"g": 0, "u": 0, "d": 0}

        def emit_ffn(call, bufs, dep, prev_mm_free):
            if PHASES == "p0":
                return {t: cnt["dv"] for t in range(16)}, cnt["dv"]
            """call in {1, 2}; bufs: dict with Hh, wgT_b, wuT_b, wdT_f.
            dep(tg): gate for x_res chunk reads. prev_mm_free: (key, n) that
            must pass before first use of p_mm banks. Returns per-chunk evac
            counts (for call 1: x_res update dv counts)."""
            Hh, wgT_b, wuT_b, wdT_f = bufs["Hh"], bufs["wgT_b"], bufs["wuT_b"], bufs["wdT_f"]
            lnA = emit_ln_half(0, dep)
            if PHASES in ("lnh0", "lnr", "lns", "lna", "lnt"):
                return {t: cnt["dv"] for t in range(16)}, cnt["dv"]
            lnB = emit_ln_half(1, dep)
            ln_dv = cnt["dv"]
            ln_ac = cnt["ac"]
            x_upd = {}
            if PHASES in ("ln1", "lnh0", "lnr", "lns", "lna", "lnt"):
                return {t: cnt["dv"] for t in range(16)}, cnt["dv"]

            gu_mm_pe = {}
            silu_ac = {}
            hmul_dv = {}
            last_down_ev = [0]

            wready = {}

            def emit_tr3(jj):
                busy2 = ("pe", gu_mm_pe[(0, jj - 3)]) if (0, jj - 3) in gu_mm_pe \
                    else None
                wg_r = wslab_transpose("g", jj, wgT_b, (jj % 3) * 1024, wgT_sc,
                                       (3 * jj) % 6, busy2)
                wu_r = wslab_transpose("u", jj, wuT_b, (jj % 3) * 1024,
                                       wuT_sc, (3 * jj + 1) % 6, busy2)
                wd_r = wslab_transpose("d", jj, wdT_f, jj * 1024, wdT_sc,
                                       (3 * jj + 2) % 6, None)
                spill_done["g"] = spill_done["u"] = spill_done["d"] = cnt["do"]
                bufs.setdefault("wd_ready", {})[jj] = wd_r
                wready[jj] = (wg_r, wu_r, wd_r)

            for hf in range(2):
                for j in range(NJ):
                    bslot = j % 3
                    jp3 = (hf, j - 3) if j >= 3 else ((hf - 1, NJ - 3 + j) if hf == 1 else None)
                    busy = ("pe", gu_mm_pe[jp3]) if jp3 in gu_mm_pe else None
                    if call == 2 and busy is None and prev_mm_free is not None:
                        busy = prev_mm_free
                    if call == 1 and hf == 0:
                        if j == 0:
                            for jj in range(3):
                                emit_tr3(jj)
                        wg_r, wu_r, wd_r = wready[j]
                    elif hf == 0:
                        wg_r = wslab_stream(wgT_sc, j, wgT_b, bslot * 1024, busy,
                                            spill_done["g"])
                        wu_r = wslab_stream(wuT_sc, j, wuT_b, bslot * 1024,
                                            busy, spill_done["u"])
                        wd_r = wslab_stream(wdT_sc, j, wdT_f, j * 1024,
                                            prev_mm_free if j == 0 else None,
                                            spill_done["d"])
                        bufs.setdefault("wd_ready", {})[j] = wd_r
                    else:
                        wg_r = wslab_stream(wgT_sc, j, wgT_b, bslot * 1024, busy,
                                            spill_done["g"])
                        wu_r = wslab_stream(wuT_sc, j, wuT_b, bslot * 1024,
                                            busy, spill_done["u"])

                    def ft(t, j=j, hf=hf, bslot=bslot, wg_r=wg_r, wu_r=wu_r,
                           ln_dv=ln_dv, ln_ac=ln_ac, first=(j == 0)):
                        w(t, wg_r[0], wg_r[1])
                        if first:
                            w(t, "dv", ln_dv)
                            w(t, "ac", ln_ac)
                            w(t, "dv", last_down_ev[0])
                            if hf == 0 and prev_mm_free is not None:
                                w(t, prev_mm_free[0], prev_mm_free[1])
                        jp = (hf, j - 1) if j >= 1 else (hf - 1, NJ - 1)
                        if jp in silu_ac:
                            w(t, "ac", silu_ac[jp])
                        li = None
                        for dc in range(8):
                            lhs = wgT_b[:, bslot * 1024 + dc * 128:
                                        bslot * 1024 + (dc + 1) * 128]
                            for blk in range(2):
                                rhs = xT[:, dc * TOK + hf * 1024 + blk * 512:
                                         dc * TOK + hf * 1024 + (blk + 1) * 512]
                                li = t.matmul(p_mm[blk][:], lhs, rhs,
                                              start=(dc == 0), stop=(dc == 7))
                        li.then_inc(S["pe"], 1)
                        cnt["pe"] += 1
                    pe(ft)
                    gate_pe = cnt["pe"]

                    def ftu(t, j=j, hf=hf, bslot=bslot, wu_r=wu_r,
                            jp=(hf, j - 1) if j >= 1 else (hf - 1, NJ - 1)):
                        w(t, wu_r[0], wu_r[1])
                        if jp in hmul_dv:
                            w(t, "dv", hmul_dv[jp])
                        li = None
                        for dc in range(8):
                            lhs = wuT_b[:, bslot * 1024 + dc * 128:
                                        bslot * 1024 + (dc + 1) * 128]
                            for blk in range(2):
                                rhs = xT[:, dc * TOK + hf * 1024 + blk * 512:
                                         dc * TOK + hf * 1024 + (blk + 1) * 512]
                                li = t.matmul(p_mm[2 + blk][:], lhs, rhs,
                                              start=(dc == 0), stop=(dc == 7))
                        li.then_inc(S["pe"], 1)
                        cnt["pe"] += 1
                    pe(ftu)
                    gu_mm_pe[(hf, j)] = cnt["pe"]
                    gu_pe = cnt["pe"]
                    if call == 1 and hf == 0 and j + 3 < NJ:
                        emit_tr3(j + 3)

                    def fs(s, n=gate_pe, hp=hmul_dv.get((hf, j - 1) if j >= 1
                                                        else (hf - 1, NJ - 1), 0)):
                        w(s, "pe", n)
                        w(s, "dv", hp)
                        li = None
                        for blk in range(2):
                            li = s.activation(sq_scr[:, blk * 512:(blk + 1) * 512],
                                              p_mm[blk][:], AF.Silu, scale=1.0 / 127.0)
                        li.then_inc(S["ac"], 1)
                        cnt["ac"] += 1
                    ac(fs)
                    silu_ac[(hf, j)] = cnt["ac"]

                    def fv(v, j=j, n_ac=cnt["ac"], n_pe=gu_pe):
                        w(v, "ac", n_ac)
                        w(v, "pe", n_pe)
                        li = None
                        for blk in range(2):
                            li = v.tensor_mul(Hh[:, j * 1024 + blk * 512:
                                                 j * 1024 + (blk + 1) * 512],
                                              sq_scr[:, blk * 512:(blk + 1) * 512],
                                              p_mm[2 + blk][:])
                        li.then_inc(S["dv"], 1)
                        cnt["dv"] += 1
                    dv(fv)
                    hmul_dv[(hf, j)] = cnt["dv"]

                if PHASES in ("hh", "wdt") and hf == 0 and call == 1:
                    def fdh(g, nd=cnt["dv"], na=cnt["ac"], ndo=cnt["do"]):
                        w(g, "dv", nd)
                        w(g, "ac", na)
                        w(g, "do", ndo)
                        src = Hh if PHASES == "hh" else wdT_f
                        v32 = src.bitcast(F32)
                        g.dma_start(
                            out=bass.AP(y_out.tensor, 0, [[12288, 128], [1, 12288]]),
                            in_=v32[:, 0:12288]).then_inc(S["dg"], 16)
                        cnt["dg"] += 16
                    gp(fdh)
                    return {t: cnt["dv"] for t in range(16)}, cnt["dv"]

                down_ev = {}
                for grp in range(4):
                    def ftd(t, hf=hf, grp=grp,
                            h_last=hmul_dv[(hf, NJ - 1)], s_last=silu_ac[(hf, NJ - 1)]):
                        if grp == 0:
                            w(t, "dv", h_last)
                            w(t, "ac", s_last)
                        else:
                            w(t, "dv", down_ev[grp - 1])
                        for j in range(NJ):
                            wd_r = bufs["wd_ready"][j]
                            w(t, wd_r[0], wd_r[1])
                            for ch in range(2):
                                tl = grp * 2 + ch
                                lhs = Hh[:, j * 1024 + tl * 128: j * 1024 + (tl + 1) * 128]
                                for dh in range(2):
                                    rhs = wdT_f[:, j * 1024 + dh * 512:
                                                j * 1024 + (dh + 1) * 512]
                                    li = t.matmul(p_mm[ch * 2 + dh][:], lhs, rhs,
                                                  start=(j == 0), stop=(j == NJ - 1))
                        li.then_inc(S["pe"], 1)
                        cnt["pe"] += 1
                    pe(ftd)
                    d_pe = cnt["pe"]

                    def fvd(v, hf=hf, grp=grp, n=d_pe, call=call):
                        w(v, "pe", n)
                        for ch in range(2):
                            tg = hf * 8 + grp * 2 + ch
                            if call == 2:
                                w(v, "do", outdma.get(tg - 2, 0))
                            for dh in range(2):
                                pmm = p_mm[ch * 2 + dh][:]
                                xsl = x_res[:, tg * D + dh * 512: tg * D + (dh + 1) * 512]
                                if call == 1:
                                    li = v.scalar_tensor_tensor(xsl, pmm, FSC, xsl,
                                                                AX.mult, AX.add)
                                else:
                                    li = v.scalar_tensor_tensor(
                                        outst[:, (tg % 2) * 1024 + dh * 512:
                                              (tg % 2) * 1024 + (dh + 1) * 512],
                                        pmm, FSC, xsl, AX.mult, AX.add)
                        li.then_inc(S["dv"], 1)
                        cnt["dv"] += 1
                    dv(fvd)
                    down_ev[grp] = cnt["dv"]
                    last_down_ev[0] = cnt["dv"]
                    for ch in range(2):
                        x_upd[hf * 8 + grp * 2 + ch] = cnt["dv"]

                    if call == 2:
                        def fyo(y, hf=hf, grp=grp, n=cnt["dv"]):
                            w(y, "dv", n)
                            for ch in range(2):
                                tg = hf * 8 + grp * 2 + ch
                                y.dma_start(
                                    out=bass.AP(y_out.tensor, tg * 128 * D,
                                                [[D, 128], [1, D]]),
                                    in_=outst[:, (tg % 2) * 1024:(tg % 2 + 1) * 1024],
                                ).then_inc(S["do"], 16)
                                cnt["do"] += 16
                                outdma[tg] = cnt["do"]
                        sy(fyo)
            return x_upd, cnt["dv"]

        outdma = {}

        ffn_es1 = ExitStack()
        if PHASES in ("ln1", "lnh0"):
            bufs1 = {
                "Hh": sb(ffn_es1, "Hh1", [128, 128]),
                "wgT_b": sb(ffn_es1, "wgT1", [128, 128]),
                "wuT_b": sb(ffn_es1, "wuT1", [128, 128]),
                "wdT_f": sb(ffn_es1, "wdT1", [128, 128]),
            }
        else:
            bufs1 = {
                "Hh": sb(ffn_es1, "Hh1", [128, NJ * 1024]),
                "wgT_b": sb(ffn_es1, "wgT1", [128, 3 * 1024]),
                "wuT_b": sb(ffn_es1, "wuT1", [128, 3 * 1024]),
                "wdT_f": sb(ffn_es1, "wdT1", [128, NJ * 1024]),
            }
        x1_upd, ffn1_dv = emit_ffn(1, bufs1, lambda tg: ("dg", x_dg), None)
        if PHASES in ("ffn1", "ln1", "lnh0", "p0", "lnr", "lns", "lna", "lnt",
                      "hh", "wdt"):
            if PHASES == "lnh0":
                def fvend(v, na=cnt["ac"]):
                    w(v, "ac", na)
                    v.tensor_copy(lnstat[:, 96:104], lnstat[:, 24:32]).then_inc(S["dv"], 1)
                    cnt["dv"] += 1
                dv(fvend)

            def fdum(g, nd=cnt["dv"], na=cnt["ac"]):
                w(g, "dv", nd)
                w(g, "ac", na)
                if PHASES in ("lnh0", "ln1"):
                    g.dma_start(
                        out=bass.AP(y_out.tensor, 0, [[8192, 128], [1, 128]]),
                        in_=lnstat[:],
                    ).then_inc(S["dg"], 16)
                    cnt["dg"] += 16
                    g.dma_start(
                        out=bass.AP(y_out.tensor, 128, [[8192, 128], [1, 2048]]),
                        in_=xnbuf[:],
                    ).then_inc(S["dg"], 16)
                    cnt["dg"] += 16
                    g.dma_start(
                        out=bass.AP(y_out.tensor, 128 + 2048, [[8192, 128], [1, 4096]]),
                        in_=xT[:, 0:4096],
                    ).then_inc(S["dg"], 16)
                    cnt["dg"] += 16
                elif PHASES not in ("hh", "wdt"):
                    g.dma_start(
                        out=bass.AP(y_out.tensor, 0, [[D, 128], [128 * D, NCHO], [1, D]]),
                        in_=bass.AP(x_res, 0, [[NCHO * D, 128], [D, NCHO], [1, D]]),
                    ).then_inc(S["dg"], 16)
                    cnt["dg"] += 16
            gp(fdum)

            def ffin1(y, n=cnt["do"], nd=cnt["dg"]):
                y.wait_ge(S["do"], n)
                y.wait_ge(S["dg"], nd)
            sy(ffin1)
            ffn_es1.close()
            return nc
        ffn_es1.close()

        x2_upd = x1_upd

        if not SKIP_ATTN:
            attn_es = ExitStack()
            wqT = sb(attn_es, "wqT", [128, 8 * 128])
            wkT = sb(attn_es, "wkT", [128, 8 * 128])
            wvT = sb(attn_es, "wvT", [128, 8 * 512])
            woT = sb(attn_es, "woT", [128, 8 * 1024])
            x1t = sb(attn_es, "x1t", [128, 3 * 1024])
            qfm = sb(attn_es, "qfm", [128, 2 * 128])
            kfm = sb(attn_es, "kfm", [128, 2 * 128])
            phi = sb(attn_es, "phi", [128, 2 * 2048])
            psi = sb(attn_es, "psi", [128, 2 * 2048])
            phiT = sb(attn_es, "phiT", [128, 2048])
            psiT = sb(attn_es, "psiT", [128, 2048])
            vts = sb(attn_es, "vts", [128, 2 * 520])
            s2m = sb(attn_es, "s2m", [128, 1024])
            Ssb = sb(attn_es, "Ssb", [128, 1536])
            o_asm8 = sb(attn_es, "o_asm8", [128, 8192])
            oT = sb(attn_es, "oT", [128, 1024])

            def emit_ln2():
                for hf in range(2):
                    lb = hf * 64
                    for t in range(8):
                        tg = hf * 8 + t
                        xc = x_res[:, tg * D:(tg + 1) * D]

                        def fg(v, t=t, xc=xc, n=x1_upd[hf * 8 + t]):
                            w(v, "dv", n)
                            v.reduce_sum(lnstat[:, lb + t:lb + t + 1], xc, axis=AXL.X)
                            v.tensor_mul(lnsq[:], xc, xc)
                            v.reduce_sum(lnstat[:, lb + 8 + t:lb + 9 + t], lnsq[:],
                                         axis=AXL.X).then_inc(S["dv"], 1)
                            cnt["dv"] += 1
                        dv(fg)

                    def fv(v):
                        mu, var = lnstat[:, lb + 16:lb + 24], lnstat[:, lb + 24:lb + 32]
                        v.drain()
                        v.tensor_scalar(mu, lnstat[:, lb + 0:lb + 8], 1.0 / D, 0.0, AX.mult, AX.add)
                        v.tensor_scalar(var, lnstat[:, lb + 8:lb + 16], 1.0 / D, 0.0, AX.mult, AX.add)
                        v.tensor_scalar(lnstat[:, lb + 56:lb + 64], lnstat[:, lb + 0:lb + 8],
                                        -1.0 / D, 0.0, AX.mult, AX.add)
                        v.drain()
                        v.tensor_mul(lnstat[:, lb + 32:lb + 40], mu, mu)
                        v.drain()
                        v.tensor_sub(var, var, lnstat[:, lb + 32:lb + 40])
                        v.drain().then_inc(S["dv"], 1)
                        cnt["dv"] += 1
                    dv(fv)

                    def fs(s, n=cnt["dv"]):
                        w(s, "dv", n)
                        s.activation(lnstat[:, lb + 40:lb + 48], lnstat[:, lb + 24:lb + 32], AF.Sqrt,
                                     bias=EPS, scale=1.0)
                        s.drain().then_inc(S["ac"], 1)
                        cnt["ac"] += 1
                    ac(fs)

                    def fv2(v, n=cnt["ac"]):
                        w(v, "ac", n)
                        inv = lnstat[:, lb + 48:lb + 56]
                        v.reciprocal(inv, lnstat[:, lb + 40:lb + 48])
                        v.drain()
                        v.tensor_mul(lnstat[:, lb + 32:lb + 40], lnstat[:, lb + 56:lb + 64],
                                     inv)
                        v.drain().then_inc(S["dv"], 1)
                        cnt["dv"] += 1
                    dv(fv2)
                    inv_dv = cnt["dv"]

                    slot_dma = {}
                    for t in range(8):
                        tg = hf * 8 + t
                        xc = x_res[:, tg * D:(tg + 1) * D]
                        slot = t % 2

                        def fv3(v, xc=xc, t=t, slot=slot, n=inv_dv,
                                pdma=slot_dma.get(t - 2, 0)):
                            w(v, "dv", n)
                            w(v, "do", pdma)
                            v.tensor_scalar(xnbuf[:, slot * 1024:(slot + 1) * 1024], xc,
                                            lnstat[:, lb + 48 + t:lb + 49 + t],
                                            lnstat[:, lb + 32 + t:lb + 33 + t],
                                            AX.mult, AX.add).then_inc(S["dv"], 1)
                            cnt["dv"] += 1
                        dv(fv3)
                        ap_dv = cnt["dv"]

                        items = [(xnbuf[:, slot * 1024 + dc * 128:
                                        slot * 1024 + (dc + 1) * 128],
                                  xnbuf, slot * 1024 + dc * 128) for dc in range(8)]
                        (ek, en), _ = tr_group(
                            items, "A" if slot == 0 else "B",
                            "dv" if slot == 0 else "ac",
                            extra_waits=[("dv", ap_dv)],
                            batch_evac=xnbuf[:, slot * 1024:(slot + 1) * 1024])

                        def fy(y, tg=tg, hf=hf, slot=slot, ek=ek, en=en):
                            w(y, ek, en)
                            y.dma_start(
                                out=bass.AP(cc_x1_in.tensor,
                                            hf * D * 1024 + (tg % 8) * 128,
                                            [[1024, 128], [128 * 1024, 8], [1, 128]]),
                                in_=xnbuf[:, slot * 1024:(slot + 1) * 1024],
                            ).then_inc(S["do"], 16)
                            cnt["do"] += 16
                        sy(fy)
                        slot_dma[t] = cnt["do"]

                    def fcc(g, hf=hf, n=cnt["do"]):
                        w(g, "do", n)
                        g.collective_compute(
                            "AllGather", AX.bypass, replica_groups=GROUPS,
                            ins=[bass.AP(cc_x1_in.tensor, hf * D * 1024,
                                         [[1024, D], [1, 1024]])],
                            outs=[cc_x1_outh[hf]]).then_inc(S["cc"], 1)
                        cnt["cc"] += 1
                    gp(fcc)

            emit_ln2()
            x1_ag = cnt["cc"]

            if PHASES == "x1dump":
                def fx1(g, n=x1_ag):
                    w(g, "cc", n)
                    g.dma_start(
                        out=bass.AP(y_out.tensor, 0, [[2048, 1024], [1, 2048]]),
                        in_=bass.AP(cc_x1_outh[0].tensor, 0, [[1024, 2048], [1, 1024]]),
                    ).then_inc(S["dg"], 16)
                    cnt["dg"] += 16
                gp(fx1)

                def ffx1(y, n=cnt["do"], nd=cnt["dg"]):
                    y.wait_ge(S["do"], n)
                    y.wait_ge(S["dg"], nd)
                sy(ffx1)
                attn_es.close()
                return nc

            def emit_attn_w():
                jobs = []
                jobs.append((bass.AP(wq_n.tensor, 0, [[D, 128], [1, D]]), wqT,
                             [(dc, dc * 128) for dc in range(8)]))
                jobs.append((bass.AP(wk_n.tensor, 0, [[D, 128], [1, D]]), wkT,
                             [(dc, dc * 128) for dc in range(8)]))
                for s_ in range(4):
                    jobs.append((bass.AP(wv_n.tensor, s_ * 128 * D, [[D, 128], [1, D]]),
                                 wvT, [(dc, dc * 512 + s_ * 128) for dc in range(8)]))
                for dcs in range(8):
                    jobs.append((bass.AP(wo_n.tensor, dcs * 128 * D, [[D, 128], [1, D]]),
                                 woT, [(ofc, ofc * 1024 + dcs * 128) for ofc in range(8)]))
                for ji, (src, dstb, tiles) in enumerate(jobs):
                    slot = ji % 6
                    dg_n = stage_load(slot, src)
                    stage = wstage[:, slot * 1024:(slot + 1) * 1024]
                    items = [(stage[:, sc * 128:(sc + 1) * 128], dstb, off)
                             for (sc, off) in tiles]
                    (_ek, _en), pe_n = tr_group(items, "A" if ji % 2 == 0 else "B",
                                                "dv" if ji % 2 == 0 else "ac",
                                                extra_waits=[("dg", dg_n)])
                    wsslot_user[slot] = ("pe", pe_n)

            emit_attn_w()

            def fvts(g, n=ffn1_dv, ng=cnt["gp"]):
                w(g, "dv", n)
                w(g, "gp", ng)
                g.memset(bass.AP(vts, 64,
                                 [[2 * 520, 128], [520, 2], [65, 8]]), 1.0).then_inc(S["gp"], 1)
                cnt["gp"] += 1
            gp(fvts)
            vts_gp = cnt["gp"]

            st = {"x1t": {}, "proj": {}, "projqk": {}, "qkc": {}, "fmsq": {}, "fm_dv": {},
                  "fm_a": {}, "fm_s": {}, "phi": {}, "psi": {}, "vev": {}, "tr": {},
                  "spe": {}, "mask": {}, "ope": {}, "oev": {}, "upd": {},
                  "sc": {}, "odma": {}, "oag": {}}

            def x1t_load(c):
                rank, h, tloc = c // 16, (c % 16) // 8, c % 8

                def fy(y, c=c, rank=rank, h=h, tloc=tloc):
                    w(y, "cc", x1_ag - 1 + h)
                    if c - 3 >= 0:
                        w(y, "pe", st["proj"][c - 3])
                    y.dma_start(
                        out=x1t[:, (c % 3) * 1024:((c % 3) + 1) * 1024],
                        in_=bass.AP(cc_x1_outh[h].tensor,
                                    rank * D * 1024 + tloc * 128,
                                    [[1024, 128], [128 * 1024, 8], [1, 128]]),
                    ).then_inc(S["dy"], 16)
                    cnt["dy"] += 16
                sy(fy)
                st["x1t"][c] = cnt["dy"]

            def proj_qk(c, extra=None):
                def ft(t, c=c, extra=extra):
                    w(t, "dy", st["x1t"][c])
                    if extra is not None:
                        w(t, extra[0], extra[1])
                    if c >= 1:
                        w(t, "dv", st["fm_a"][c - 1])
                    for dc in range(8):
                        lhs = x1t[:, (c % 3) * 1024 + dc * 128:
                                  (c % 3) * 1024 + (dc + 1) * 128]
                        t.matmul(p_qk[:, 0:128], lhs, wqT[:, dc * 128:(dc + 1) * 128],
                                 start=(dc == 0), stop=False)
                        li = t.matmul(p_qk[:, 128:256], lhs,
                                      wkT[:, dc * 128:(dc + 1) * 128],
                                      start=False, stop=(dc == 7))
                    li.then_inc(S["pe"], 1)
                    cnt["pe"] += 1
                pe(ft)
                st["projqk"][c] = cnt["pe"]

            def proj_v(c):
                def ft(t, c=c):
                    w(t, "dy", st["x1t"][c])
                    if c >= 1:
                        w(t, "dv", st["mask"][(c - 1, 1)])
                    for dc in range(8):
                        lhs = x1t[:, (c % 3) * 1024 + dc * 128:
                                  (c % 3) * 1024 + (dc + 1) * 128]
                        li = t.matmul(p_vs[:], lhs, wvT[:, dc * 512:(dc + 1) * 512],
                                      start=(dc == 0), stop=(dc == 7))
                    li.then_inc(S["pe"], 1)
                    cnt["pe"] += 1
                pe(ft)
                st["proj"][c] = cnt["pe"]

            def fm_a(c):
                def fvA(v, c=c):
                    w(v, "pe", st["projqk"][c])
                    v.tensor_copy(qkbuf[:], p_qk[:, 0:256])
                    qk3 = qkbuf[:].rearrange("p (h a) -> p h a", h=16)
                    v.reduce_sum(fmst[:, 0:16], qk3, axis=AXL.X)
                    v.tensor_mul(sq_scr[:, 0:256], qkbuf[:], qkbuf[:])
                    v.reduce_sum(fmst[:, 16:32],
                                 sq_scr[:, 0:256].rearrange("p (h a) -> p h a", h=16),
                                 axis=AXL.X).then_inc(S["dv"], 1)
                    cnt["dv"] += 1
                dv(fvA)
                st["fm_a"][c] = cnt["dv"]

            def fm_s(c):
                def fvS(v, c=c):
                    mu, var = fmst[:, 32:48], fmst[:, 48:64]
                    v.drain()
                    v.tensor_scalar(mu, fmst[:, 0:16], 1.0 / FD, 0.0, AX.mult, AX.add)
                    v.tensor_scalar(var, fmst[:, 16:32], 1.0 / FD, 0.0, AX.mult, AX.add)
                    v.drain()
                    v.tensor_mul(fmst[:, 64:80], mu, mu)
                    v.drain()
                    v.tensor_sub(var, var, fmst[:, 64:80])
                    v.drain().then_inc(S["dv"], 1)
                    cnt["dv"] += 1
                dv(fvS)
                st["fm_s"][c] = cnt["dv"]

            def fm_b(c):
                def fa2(s_, n=st["fm_s"][c]):
                    w(s_, "dv", n)
                    s_.activation(fmst[:, 48:64], fmst[:, 48:64], AF.Sqrt,
                                  bias=EPS, scale=1.0)
                    s_.drain().then_inc(S["ac"], 1)
                    cnt["ac"] += 1
                ac(fa2)

                def fv3(v, c=c, n=cnt["ac"]):
                    w(v, "ac", n)
                    inv = fmst[:, 80:96]
                    v.reciprocal(inv, fmst[:, 48:64])
                    v.drain()
                    slot = (c % 2) * 128
                    for which, (go, bo, dstb) in enumerate(
                            ((0, FD, qfm), (2 * FD, 3 * FD, kfm))):
                        mu_b = bass.AP(fmst, 32 + which * 8, [[96, 128], [1, 8], [0, 16]])
                        inv_b = bass.AP(fmst, 80 + which * 8, [[96, 128], [1, 8], [0, 16]])
                        g_b = bass.AP(gb, go, [[4 * FD, 128], [0, 8], [1, 16]])
                        b_b = bass.AP(gb, bo, [[4 * FD, 128], [0, 8], [1, 16]])
                        dst = sq_scr[:, 512 + which * 128: 512 + (which + 1) * 128]
                        d3 = dst.rearrange("p (h a) -> p h a", h=8)
                        v.tensor_sub(d3, qkbuf[:, which * 128:(which + 1) * 128]
                                     .rearrange("p (h a) -> p h a", h=8), mu_b)
                        v.tensor_mul(d3, d3, inv_b)
                        v.tensor_mul(d3, d3, g_b)
                        li = v.tensor_add(dstb[:, slot:slot + 128]
                                          .rearrange("p (h a) -> p h a", h=8), d3, b_b)
                    li.then_inc(S["dv"], 1)
                    cnt["dv"] += 1
                dv(fv3)
                st["fm_dv"][c] = cnt["dv"]

            def vevac(c):
                def fv(v, c=c):
                    w(v, "pe", st["proj"][c])
                    dst = bass.AP(vts, (c % 2) * 520, [[2 * 520, 128], [65, 8], [1, 64]])
                    v.tensor_copy(dst, p_vs[:].rearrange("p (h d) -> p h d",
                                                         h=8)).then_inc(S["dv"], 1)
                    cnt["dv"] += 1
                dv(fv)
                st["vev"][c] = cnt["dv"]

            def construct(c):
                slot = c % 2

                def fv(v, c=c, slot=slot):
                    w(v, "dv", st["fm_dv"][c])
                    in0 = bass.AP(qfm, slot * 128, [[256, 128], [16, 8], [1, 16], [0, 16]])
                    in1 = bass.AP(qfm, slot * 128, [[256, 128], [16, 8], [0, 16], [1, 16]])
                    v.tensor_tensor(
                        bass.AP(phi, slot * 2048,
                                [[2 * 2048, 128], [256, 8], [16, 16], [1, 16]]),
                        in0, in1, AX.mult).then_inc(S["dv"], 1)
                    cnt["dv"] += 1
                dv(fv)
                st["phi"][c] = cnt["dv"]

                def fg(v, c=c, slot=slot):
                    in0 = bass.AP(kfm, slot * 128, [[256, 128], [16, 8], [1, 16], [0, 16]])
                    in1 = bass.AP(kfm, slot * 128, [[256, 128], [16, 8], [0, 16], [1, 16]])
                    v.tensor_tensor(
                        bass.AP(psi, slot * 2048,
                                [[2 * 2048, 128], [256, 8], [16, 16], [1, 16]]),
                        in0, in1, AX.mult).then_inc(S["dv"], 1)
                    cnt["dv"] += 1
                dv(fg)
                st["psi"][c] = cnt["dv"]

            def transposes(c):
                slot = c % 2
                srcs = [bass.AP(phi, slot * 2048 + k * 128, [[2 * 2048, 128], [1, 128]])
                        for k in range(16)]
                srcs += [bass.AP(psi, slot * 2048 + k * 128, [[2 * 2048, 128], [1, 128]])
                         for k in range(16)]
                dsts = [(phiT, k * 128) for k in range(16)] + \
                       [(psiT, k * 128) for k in range(16)]
                for gi in range(4):
                    items = [(srcs[gi * 8 + k], dsts[gi * 8 + k][0], dsts[gi * 8 + k][1])
                             for k in range(8)]
                    ew = []
                    if gi == 0:
                        ew = [("dv", st["psi"][c])]
                    dstb = (phiT, psiT)[gi // 2]
                    boff = (gi % 2) * 1024
                    tr_group(items, "A" if gi % 2 == 0 else "B",
                             "dv" if gi % 2 == 0 else "ac", extra_waits=ew,
                             batch_evac=dstb[:, boff:boff + 1024])
                st["tr"][c] = cnt["pe"]
                st["trlast"] = (kk.trfree["A"], kk.trfree["B"])

            def srounds(c, r):
                def ft(t, c=c, r=r):
                    if r == 0:
                        w(t, kk.trfree["A"][0], kk.trfree["A"][1])
                        w(t, kk.trfree["B"][0], kk.trfree["B"][1])
                        w(t, "dv", st["vev"][c])
                    else:
                        w(t, "dv", st["mask"][(c, 0)])
                    for hh in range(4):
                        h = r * 4 + hh
                        for half in range(2):
                            ti = h * 2 + half
                            li = t.matmul(p_vs[:, hh * 128:(hh + 1) * 128],
                                          psiT[:, ti * 128:(ti + 1) * 128],
                                          phiT[:, ti * 128:(ti + 1) * 128],
                                          start=(hh == 0 and half == 0),
                                          stop=(hh == 3 and half == 1))
                    li.then_inc(S["pe"], 1)
                    cnt["pe"] += 1
                pe(ft)
                st["spe"][(c, r)] = cnt["pe"]

            def mask(c, r):
                def fv(v, c=c, r=r):
                    w(v, "pe", st["spe"][(c, r)])
                    w(v, "gp", const_gp)
                    tri_b = bass.AP(trimask, 0, [[128, 128], [0, 4], [1, 128]])
                    v.tensor_tensor(
                        s2m[:, r * 512:(r + 1) * 512].rearrange("p (h i) -> p h i", h=4),
                        p_vs[:].rearrange("p (h i) -> p h i", h=4),
                        tri_b, AX.mult).then_inc(S["dv"], 1)
                    cnt["dv"] += 1
                dv(fv)
                st["mask"][(c, r)] = cnt["dv"]

            def oround(c, r):
                def ft(t, c=c, r=r):
                    if c == 0:
                        w(t, "gp", vts_gp)
                    w(t, "dv", st["mask"][(c, r)])
                    if c >= 1:
                        w(t, "dv", st["sc"][c - 1])
                    if r == 1:
                        w(t, "dv", st["oev"][(c, 0)])
                    elif c >= 1:
                        w(t, "dv", st["oev"][(c - 1, 1)])
                    for hh in range(4):
                        h = r * 4 + hh
                        vsl = bass.AP(vts, (c % 2) * 520 + h * 65, [[2 * 520, 128], [1, 65]])
                        li = t.matmul(p_o[:, hh * 65:(hh + 1) * 65],
                                      s2m[:, (r * 4 + hh) * 128:(r * 4 + hh + 1) * 128],
                                      vsl, start=(hh == 0), stop=(c == 0 and hh == 3))
                        if c >= 1:
                            for half in range(2):
                                ti = h * 2 + half
                                sb_, sl = ti // 7, ti % 7
                                li = t.matmul(p_o[:, hh * 65:(hh + 1) * 65],
                                              phiT[:, ti * 128:(ti + 1) * 128],
                                              bass.AP(Ssb, sb_ * 512 + sl * 65,
                                                      [[1536, 128], [1, 65]]),
                                              start=False, stop=(hh == 3 and half == 1))
                    li.then_inc(S["pe"], 1)
                    cnt["pe"] += 1
                pe(ft)
                st["ope"][(c, r)] = cnt["pe"]

            def upd(c, r):
                def ft(t, c=c, r=r):
                    w(t, "dv", st["psi"][c])
                    for hh in range(4):
                        h = r * 4 + hh
                        vsl = bass.AP(vts, (c % 2) * 520 + h * 65, [[2 * 520, 128], [1, 65]])
                        for half in range(2):
                            ti = h * 2 + half
                            sb_, sl = ti // 7, ti % 7
                            li = t.matmul(bass.AP(p_S[sb_], sl * 65, [[512, 128], [1, 65]]),
                                          bass.AP(psi, (c % 2) * 2048 + ti * 128,
                                                  [[2 * 2048, 128], [1, 128]]),
                                          vsl,
                                          start=(c == 0 and sl == 0),
                                          stop=(c == NCH - 1 and ti in (6, 13, 15)),
                                          skip_group_check=True)
                    li.then_inc(S["pe"], 1)
                    cnt["pe"] += 1
                pe(ft)
                st["upd"][(c, r)] = cnt["pe"]

            def o_evac(c, r):
                def fv(v, c=c, r=r):
                    w(v, "pe", st["ope"][(c, r)])
                    if c >= 2 and r == 0:
                        w(v, "do", st["odma"][c - 2])
                    zv = bass.AP(p_o, 64, [[512, 128], [65, 4]])
                    v.tensor_scalar(zbuf[:, 0:4], zv, 1.0, EPS, AX.mult, AX.add)
                    v.drain()
                    v.reciprocal(zbuf[:, 4:8], zbuf[:, 0:4])
                    v.drain()
                    ov = bass.AP(p_o, 0, [[512, 128], [65, 4], [1, 64]])
                    zr_b = bass.AP(zbuf, 4, [[16, 128], [1, 4], [0, 64]])
                    v.tensor_tensor(
                        bass.AP(oev, (c % 2) * 512 + r * 256,
                                [[2 * 512, 128], [64, 4], [1, 64]]),
                        ov, zr_b, AX.mult).then_inc(S["dv"], 1)
                    cnt["dv"] += 1
                dv(fv)
                st["oev"][(c, r)] = cnt["dv"]

            def snapshot(c):
                def fv(v, c=c):
                    w(v, "pe", st["upd"][(c, 1)])
                    for i, ncol in ((0, 455), (1, 455), (2, 130)):
                        li = v.tensor_copy(Ssb[:, i * 512:i * 512 + ncol],
                                           p_S[i][:, 0:ncol])
                    li.then_inc(S["dv"], 1)
                    cnt["dv"] += 1
                dv(fv)
                st["sc"][c] = cnt["dv"]

            def o_dma(c):
                def fy(y, c=c):
                    w(y, "dv", st["oev"][(c, 1)])
                    y.dma_start(
                        out=bass.AP(cc_o_in.tensor, c * 128 * 512, [[512, 128], [1, 512]]),
                        in_=oev[:, (c % 2) * 512:((c % 2) + 1) * 512],
                    ).then_inc(S["do"], 16)
                    cnt["do"] += 16
                sy(fy)
                st["odma"][c] = cnt["do"]

            if PHASES == "x1t3":
                for c in range(3):
                    x1t_load(c)

                def fxp(g, ndy=cnt["dy"]):
                    w(g, "dy", ndy)
                    g.dma_start(
                        out=bass.AP(y_out.tensor, 0, [[16384, 128], [1, 3072]]),
                        in_=x1t[:, 0:3072]).then_inc(S["dg"], 16)
                    cnt["dg"] += 16
                gp(fxp)

                def ffxp(y, n=cnt["do"], nd=cnt["dg"]):
                    y.wait_ge(S["do"], n)
                    y.wait_ge(S["dg"], nd)
                sy(ffxp)
                attn_es.close()
                return nc

            for c in range(3):
                x1t_load(c)
            proj_qk(0, extra=("dv", ffn1_dv))
            proj_v(0)
            fm_a(0)
            fm_s(0)
            fm_b(0)
            vevac(0)
            construct(0)
            for c in range(1, NCH + 1):
                cp = c - 1
                if c < NCH:
                    proj_qk(c)
                transposes(cp)
                srounds(cp, 0)
                mask(cp, 0)
                if c < NCH:
                    fm_a(c)
                srounds(cp, 1)
                mask(cp, 1)
                if c < NCH:
                    fm_s(c)
                oround(cp, 0)
                upd(cp, 0)
                o_evac(cp, 0)
                if c < NCH:
                    fm_b(c)
                    construct(c)
                if c < NCH:
                    proj_v(c)
                    if c + 2 < NCH and (c + 2) not in st["x1t"]:
                        x1t_load(c + 2)
                oround(cp, 1)
                upd(cp, 1)
                o_evac(cp, 1)
                snapshot(cp)
                o_dma(cp)
                if PHASES == "a0" and c == 1:
                    def fa0(g, nd=cnt["dv"], na=cnt["ac"], np_=cnt["pe"],
                            ng=cnt["gp"], ndo=cnt["do"], ndy=cnt["dy"]):
                        w(g, "dv", nd)
                        w(g, "ac", na)
                        w(g, "pe", np_)
                        w(g, "gp", ng)
                        w(g, "do", ndo)
                        w(g, "dy", ndy)
                        dumps = [(x1t, 0, 1024), (qfm, 0, 128), (kfm, 0, 128),
                                 (phi, 0, 2048), (psi, 0, 2048), (s2m, 0, 1024),
                                 (vts, 0, 520), (oev, 0, 512), (zbuf, 0, 16),
                                 (phiT, 0, 2048), (psiT, 0, 2048)]
                        off = 0
                        for buf, so, ncol in dumps:
                            g.dma_start(
                                out=bass.AP(y_out.tensor, off,
                                            [[16384, 128], [1, ncol]]),
                                in_=buf[:, so:so + ncol]).then_inc(S["dg"], 16)
                            cnt["dg"] += 16
                            off += ncol
                    gp(fa0)

                    def ffa0(y, n=cnt["do"], nd=cnt["dg"]):
                        y.wait_ge(S["do"], n)
                        y.wait_ge(S["dg"], nd)
                    sy(ffa0)
                    attn_es.close()
                    return nc
                if c < NCH:
                    vevac(c)
                if cp % 8 == 7:
                    q = cp // 8

                    def fcc2(g, q=q, n=st["odma"][cp]):
                        w(g, "do", n)
                        g.collective_compute(
                            "AllGather", AX.bypass, replica_groups=GROUPS,
                            ins=[bass.AP(cc_o_in.tensor, q * 1024 * 512,
                                         [[512, 1024], [1, 512]])],
                            outs=[cc_o_out[q]]).then_inc(S["cc"], 1)
                        cnt["cc"] += 1
                    gp(fcc2)
                    st["oag"][q] = cnt["cc"]

            for c in range(NCH):
                assert c in st["x1t"], f"x1t chunk {c} never loaded"

            if PHASES == "odump":
                def fod(g, n=st["odma"][NCH - 1]):
                    w(g, "do", n)
                    g.dma_start(
                        out=bass.AP(y_out.tensor, 0, [[512, 4096], [1, 512]]),
                        in_=bass.AP(cc_o_in.tensor, 0, [[512, 4096], [1, 512]]),
                    ).then_inc(S["dg"], 16)
                    cnt["dg"] += 16
                gp(fod)

                def ffo(y, n=cnt["do"], nd=cnt["dg"]):
                    y.wait_ge(S["do"], n)
                    y.wait_ge(S["dg"], nd)
                sy(ffo)
                attn_es.close()
                return nc

            scan_pe = cnt["pe"]
            wo_pe = {}
            parv = nc.axis_index(bass.AxisInfo(2, 1))
            pconds = [parv == 0, parv == 1]

            def fpar(g):
                g.compute_val(pconds[0])
                g.compute_val(pconds[1])
            gp(fpar)
            for qi in range(2):
                def fga(g, qi=qi):
                    for pv in range(2):
                        q = pv * 2 + qi
                        w(g, "cc", st["oag"][q])
                        if qi == 1:
                            w(g, "pe", wo_pe[7])
                        for rank in range(2):
                            g.dma_start(
                                out=bass.AP(o_asm8, rank * 512,
                                            [[8192, 128], [1024, 8], [1, 512]]),
                                in_=bass.AP(cc_o_out[q].tensor, rank * 1024 * 512,
                                            [[512, 128], [128 * 512, 8], [1, 512]]),
                                cond=pconds[pv],
                            ).then_inc(S["dg"], 16)
                            cnt["dg"] += 16
                gp(fga)
                asm_dg = cnt["dg"]

                for tl in range(8):
                    t_ = qi * 8 + tl
                    items = [(o_asm8[:, tl * 1024 + ofc * 128:
                                     tl * 1024 + (ofc + 1) * 128], oT, ofc * 128)
                             for ofc in range(8)]
                    ew = [("dg", asm_dg)]
                    if t_ == 0:
                        ew.append(("pe", scan_pe))
                        ew.append(("dv", st["sc"][NCH - 1]))
                    (ek, en), _pe_n = tr_group(items, "A" if t_ % 2 == 0 else "B",
                                               "dv" if t_ % 2 == 0 else "ac",
                                               extra_waits=ew, batch_evac=oT[:])

                    def ft2(t, t_=t_, ek=ek, en=en,
                            pf=0 if t_ == 0 else None):
                        w(t, ek, en)
                        if t_ >= 1:
                            w(t, "dv", x2_upd.get(t_ - 1, 0))
                        for ofc in range(8):
                            lhs = oT[:, ofc * 128:(ofc + 1) * 128]
                            for dh in range(2):
                                li = t.matmul(p_mm[2 + dh][:], lhs,
                                              woT[:, ofc * 1024 + dh * 512:
                                                  ofc * 1024 + (dh + 1) * 512],
                                              start=(ofc == 0), stop=(ofc == 7))
                        li.then_inc(S["pe"], 1)
                        cnt["pe"] += 1
                    pe(ft2)
                    wo_pe[t_] = cnt["pe"]

                    def fvr(v, t_=t_, n=cnt["pe"]):
                        w(v, "pe", n)
                        for dh in range(2):
                            xsl = x_res[:, t_ * D + dh * 512: t_ * D + (dh + 1) * 512]
                            li = v.tensor_add(xsl, xsl, p_mm[2 + dh][:])
                        li.then_inc(S["dv"], 1)
                        cnt["dv"] += 1
                    dv(fvr)
                    x2_upd[t_] = cnt["dv"]

            attn_es.close()
            ffn2_mm_free = ("dv", cnt["dv"])
        else:
            ffn2_mm_free = ("dv", ffn1_dv)

        if PHASES == "attno":
            def fdat(g, nd=cnt["dv"], na=cnt["ac"]):
                w(g, "dv", nd)
                w(g, "ac", na)
                g.dma_start(
                    out=bass.AP(y_out.tensor, 0, [[D, 128], [128 * D, NCHO], [1, D]]),
                    in_=bass.AP(x_res, 0, [[NCHO * D, 128], [D, NCHO], [1, D]]),
                ).then_inc(S["dg"], 16)
                cnt["dg"] += 16
            gp(fdat)

            def ffat(y, n=cnt["do"], nd=cnt["dg"]):
                y.wait_ge(S["do"], n)
                y.wait_ge(S["dg"], nd)
            sy(ffat)
            return nc

        ffn_es2 = ExitStack()
        bufs2 = {
            "Hh": sb(ffn_es2, "Hh2", [128, NJ * 1024]),
            "wgT_b": sb(ffn_es2, "wgT2", [128, 3 * 1024]),
            "wuT_b": sb(ffn_es2, "wuT2", [128, 3 * 1024]),
            "wdT_f": sb(ffn_es2, "wdT2", [128, NJ * 1024]),
        }
        emit_ffn(2, bufs2, lambda tg: ("dv", x2_upd[tg]), ffn2_mm_free)
        ffn_es2.close()

        def ffin(y, n=cnt["do"]):
            y.wait_ge(S["do"], n)
        sy(ffin)

    return nc


def kernel(L0_attn_x, L0_attn_Wq, L0_attn_Wk, L0_attn_Wv, L0_attn_Wo,
           L0_attn_gamma, L0_attn_beta, L0_ffn_w_gate, L0_ffn_w_up, L0_ffn_w_down):
    x = np.ascontiguousarray(np.asarray(L0_attn_x, dtype=np.float32))
    wq = np.ascontiguousarray(np.asarray(L0_attn_Wq, dtype=np.float32))
    wk = np.ascontiguousarray(np.asarray(L0_attn_Wk, dtype=np.float32))
    wv = np.ascontiguousarray(np.asarray(L0_attn_Wv, dtype=np.float32))
    wo = np.ascontiguousarray(np.asarray(L0_attn_Wo, dtype=np.float32))
    gam = np.ascontiguousarray(np.asarray(L0_attn_gamma, dtype=np.float32))
    bet = np.ascontiguousarray(np.asarray(L0_attn_beta, dtype=np.float32))
    wg = np.ascontiguousarray(np.asarray(L0_ffn_w_gate, dtype=np.int32))
    wu = np.ascontiguousarray(np.asarray(L0_ffn_w_up, dtype=np.int32))
    wd = np.ascontiguousarray(np.asarray(L0_ffn_w_down, dtype=np.int32))

    nc = build_nc()
    in_maps = []
    for c in range(8):
        b, p = c // 2, c % 2
        in_maps.append({
            "x_own": np.ascontiguousarray(x[b, p * TOK:(p + 1) * TOK, :]),
            "wq_own": np.ascontiguousarray(wq[p * 128:(p + 1) * 128, :]),
            "wk_own": np.ascontiguousarray(wk[p * 128:(p + 1) * 128, :]),
            "wv_own": np.ascontiguousarray(wv[p * 512:(p + 1) * 512, :]),
            "wo": wo, "gamma": gam, "beta": bet,
            "w_gate": wg, "w_up": wu, "w_down": wd,
        })
    res = run_bass_kernel_spmd(nc, in_maps, core_ids=list(range(8)))
    out = np.empty((B, L, D), dtype=np.float32)
    for c in range(8):
        b, p = c // 2, c % 2
        out[b, p * TOK:(p + 1) * TOK, :] = res.results[c]["y_own"]
    return out
